# version 05
# baseline (speedup 1.0000x reference)
import contextlib
import numpy as np
import concourse.bass as bass
import concourse.mybir as mybir
from concourse.bass_utils import run_bass_kernel_spmd

F32 = mybir.dt.float32
BF16 = mybir.dt.bfloat16
I32 = mybir.dt.int32
AF = mybir.ActivationFunctionType
ALU = mybir.AluOpType
AX = mybir.AxisListType

TOK = 2048
D = 1024
INNER = 2048
E2 = 4096
DFF = 4096
NE = 8
NT = TOK // 128
ST = 512
NST = TOK // ST
EPS = 1e-5
NKD = D // 128
NKI = INNER // 128
NMU = E2 // 128
NMJ = INNER // 128
NMD = D // 128
NMF = DFF // 128


class Ctr:
    def __init__(self, handle):
        self.h = handle
        self.n = 0

    def inc(self, instr, k=1):
        instr.then_inc(self.h, k)
        self.n += k
        return self.n


def build(debug=False, upto=99):
    nc = bass.Bass("TRN2", target_bir_lowering=False)
    nc.detect_race_conditions = False

    x_h = nc.dram_tensor("x", [TOK, D], F32, kind="ExternalInput").ap()
    wn_h = nc.dram_tensor("w_norm", [D], F32, kind="ExternalInput").ap()
    win_h = nc.dram_tensor("W_in", [E2, D], F32, kind="ExternalInput").ap()
    wdt_h = nc.dram_tensor("W_dt", [INNER, INNER], F32, kind="ExternalInput").ap()
    bdt_h = nc.dram_tensor("b_dt", [INNER], F32, kind="ExternalInput").ap()
    wout_h = nc.dram_tensor("W_out", [D, INNER], F32, kind="ExternalInput").ap()
    w1_h = nc.dram_tensor("W1", [NE, D, DFF], F32, kind="ExternalInput").ap()
    b1_h = nc.dram_tensor("b1", [NE, DFF], F32, kind="ExternalInput").ap()
    w2_h = nc.dram_tensor("W2", [NE, DFF, D], F32, kind="ExternalInput").ap()
    b2_h = nc.dram_tensor("b2", [NE, D], F32, kind="ExternalInput").ap()
    eye_h = nc.dram_tensor("eye", [128, 128], F32, kind="ExternalInput").ap()

    skind = "ExternalOutput" if debug else "Internal"
    ubf_s = nc.dram_tensor("s_ubf", [NST, 128, NKI * ST], BF16, kind=skind).ap()
    uf_s = nc.dram_tensor("s_uf", [NST, 128, NKI * ST], F32, kind=skind).ap()
    szb_s = nc.dram_tensor("s_szb", [NST, 128, NKI * ST], BF16, kind=skind).ap()
    xnb_s = nc.dram_tensor("s_xnb", [NST, 128, NKD * ST], BF16, kind=skind).ap()
    ya_s = nc.dram_tensor("s_ya", [NST, 128, NKD * ST], F32, kind=skind).ap()
    rows_s = nc.dram_tensor("s_rows", [2, TOK], F32, kind=skind).ap()
    out_h = nc.dram_tensor("out", [128, NST * NMD * ST], F32,
                           kind="ExternalOutput").ap()
    mask_s = nc.dram_tensor("s_mask", [NE, TOK], F32, kind=skind).ap()
    w1b_s = nc.dram_tensor("s_w1b", [NE, 128, NKD * DFF], BF16, kind="Internal").ap()
    dbg_rows = (nc.dram_tensor("s_dbgrows", [8, TOK], F32, kind="ExternalOutput").ap()
                if debug else None)
    dbg_pk = (nc.dram_tensor("s_dbgpk", [128, 4 * NT], F32, kind="ExternalOutput").ap()
              if debug else None)

    gs = contextlib.ExitStack()
    with gs:
        def sb(name, shape, dt, stack=gs):
            return stack.enter_context(nc.sbuf_tensor(name, shape, dt))

        def mksem(name):
            return Ctr(gs.enter_context(nc.semaphore(name)))

        eye_f = sb("eye_f", [128, 128], F32)
        eye_b = sb("eye_b", [128, 128], BF16)
        ones_f = sb("ones_f", [1, 128], F32)
        ones_b = sb("ones_b", [1, 128], BF16)
        wn_sb = sb("wn_sb", [128, NKD], F32)
        bdt_sb = sb("bdt_sb", [128, NKI], F32)
        b1_sb = sb("b1_sb", [128, NE * NMF], F32)
        b2_sb = sb("b2_sb", [NE, D], F32)
        state = sb("state", [128, NKI], F32)
        p = [gs.enter_context(nc.psum_tensor(f"pb{i}", [128, 512], F32))
             for i in range(8)]

        qC = mksem("qC")
        qE = mksem("qE")
        sX = [mksem(f"sX{i}") for i in range(4)]
        sW = [mksem(f"sW{i}") for i in range(2)]
        sXS = mksem("sXS")
        sXN = mksem("sXN")
        sUF = mksem("sUF")
        sSG = mksem("sSG")
        sRC = mksem("sRC")
        sRT = [mksem(f"sRT{i}") for i in range(2)]
        sIN4 = mksem("sIN4")
        sYA = mksem("sYA")
        sP5 = mksem("sP5")
        sPRE = mksem("sPRE")
        sXB = [mksem(f"sXB{i}") for i in range(2)]
        sW1 = mksem("sw1ld")
        sW2 = [mksem(f"sW2{i}") for i in range(2)]
        sMK = mksem("sMK")
        sCB = [mksem(f"sCB{i}") for i in range(2)]
        sCW = [mksem(f"sCW{i}") for i in range(2)]
        qO = [mksem(f"qO{i}") for i in range(2)]
        cT = mksem("cT")
        cV = mksem("cV")
        cS = mksem("cS")
        ENAMES = ("sync", "gpsimd", "vector", "scalar", "tensor")
        bar = {e: mksem("bar_" + e) for e in ENAMES}
        nbar = [0]

        block = gs.enter_context(nc.Block())
        B = {"sync": block.sync, "gpsimd": block.gpsimd, "vector": block.vector,
             "scalar": block.scalar, "tensor": block.tensor}

        def finish():
            barrier()
            def syfin(sy):
                for sm in (qC, qE, sXS, sXN, sUF, sSG, sRC, sIN4, sYA, sP5,
                           sPRE, sW1, sMK):
                    sy.wait_ge(sm.h, sm.n)
                for sm in sX + sW + sRT + sXB + sW2 + qO:
                    sy.wait_ge(sm.h, sm.n)
            B["sync"](syfin)

        def barrier():
            nbar[0] += 1
            k = nbar[0]
            for name in ENAMES:
                def f(eng, name=name, k=k):
                    eng.sem_inc(bar[name].h, 1)
                    for other in ENAMES:
                        if other != name:
                            eng.wait_ge(bar[other].h, k)
                B[name](f)
                bar[name].n = k

        p12 = contextlib.ExitStack()
        p12.__enter__()
        winT = sb("winT", [128, NKD * NMU * 128], BF16, p12)
        r_nmean = sb("r_nmean", [1, TOK], F32, p12)
        r_rstd = sb("r_rstd", [1, TOK], F32, p12)
        r_rstdh = sb("r_rstdh", [1, TOK], F32, p12)
        cst_rows = sb("cst_rows", [16, 128], F32, p12)
        wn_st = sb("wn_st", [8, 128], F32, p12)
        b1st = sb("b1st", [32, NE * 128], F32, p12)

        def p0_scalar(s):
            qC.inc(s.dma_start(out=eye_f[:], in_=eye_h), 16)
            qC.inc(s.dma_start(out=cst_rows[0:16, :],
                               in_=bdt_h.rearrange("(t q) -> t q", q=128)), 16)
            qC.inc(s.dma_start(out=wn_st[:],
                               in_=wn_h.rearrange("(t q) -> t q", q=128)), 16)
            for e in range(NE):
                qC.inc(s.dma_start(out=b1st[:, e * 128:(e + 1) * 128],
                                   in_=b1_h[e].rearrange("(t q) -> t q", q=128)), 16)
            qC.inc(s.dma_start(out=b2_sb[:], in_=b2_h), 16)
        B["scalar"](p0_scalar)

        def p0_g(g):
            qE.inc(g.dma_start(out=eye_b[:], in_=eye_h), 16)
        B["gpsimd"](p0_g)

        def p0_v(v):
            v.memset(ones_f[:], 1.0)
            v.memset(ones_b[:], 1.0)
            cV.inc(v.memset(state[:], 0.0))
        B["vector"](p0_v)

        p1 = contextlib.ExitStack()
        with p1:
            NWB = 2
            wst = [sb(f"wst{i}", [128, D], F32, p1) for i in range(NWB)]
            NXB = 4
            x_tl = [sb(f"x_tl{i}", [128, D], F32, p1) for i in range(NXB)]
            sq_tl = sb("sq_tl", [128, D], F32, p1)
            packed = sb("packed", [128, 4 * NT], F32, p1)
            stt = sb("stt", [128, 12], F32, p1)
            ic_t = sb("ic_t", [128, 1], I32, p1)
            r_route = sb("r_route", [1, TOK], F32, p1)
            rowt = [sb(f"rowt{i}", [1, TOK], F32, p1) for i in range(2)]
            rowt2 = sb("rowt2", [1, TOK], F32, p1)
            rows4 = sb("rows4", [4, TOK], F32, p1)

            def pk(i, c):
                return packed[:, 4 * i + c:4 * i + c + 1]

            qc_all = qC.n

            def tCst(t):
                t.wait_ge(qC.h, qc_all)
                cT.inc(t.transpose(p[0][0:128, 0:16], cst_rows[0:16, :], eye_f[0:16, 0:16]))
                cT.inc(t.transpose(p[1][0:128, 0:8], wn_st[:], eye_f[0:8, 0:8]))
                for e in range(NE):
                    bank = p[2 + e // 4]
                    off = (e % 4) * 32
                    cT.inc(t.transpose(bank[0:128, off:off + 32],
                                       b1st[:, e * 128:(e + 1) * 128],
                                       eye_f[0:32, 0:32]))
            B["tensor"](tCst)
            ct_cst = cT.n

            def vCst(v):
                v.wait_ge(cT.h, ct_cst)
                v.tensor_copy(bdt_sb[:], p[0][0:128, 0:16])
                v.tensor_copy(wn_sb[:], p[1][0:128, 0:8])
                for e in range(NE):
                    bank = p[2 + e // 4]
                    off = (e % 4) * 32
                    v.tensor_copy(b1_sb[:, e * NMF:(e + 1) * NMF],
                                  bank[0:128, off:off + 32])
                cV.inc(v.engine_nop())
            B["vector"](vCst)
            cv_cst = cV.n

            strip_done_v = [0] * NMU
            grp_evac_v = [0] * (NMU * 2)

            for mj in range(NMU):
                def gA(g, mj=mj):
                    if mj == 0:
                        g.wait_ge(qC.h, 16)
                    if mj >= NWB:
                        g.wait_ge(cV.h, strip_done_v[mj - NWB])
                    sW[mj % NWB].inc(g.dma_start(out=wst[mj % NWB][:],
                                     in_=win_h[mj * 128:(mj + 1) * 128, :]), 16)
                B["gpsimd"](gA)
                qg_strip = sW[mj % NWB].n

                for gi in range(2):
                    def tA(t, mj=mj, gi=gi, qg_strip=qg_strip):
                        if gi == 0:
                            t.wait_ge(sW[mj % NWB].h, qg_strip)
                        gidx = mj * 2 + gi
                        if gidx >= 1:
                            t.wait_ge(cV.h, grp_evac_v[gidx - 1])
                        mi = None
                        for kk in range(4):
                            ki = gi * 4 + kk
                            bank = p[4 + kk]
                            mi = t.transpose(bank[0:128, 0:128],
                                             wst[mj % NWB][:, ki * 128:(ki + 1) * 128],
                                             eye_f[:])
                        cT.inc(mi)
                    B["tensor"](tA)
                    ct_g = cT.n

                    def vA(v, mj=mj, gi=gi, ct_g=ct_g):
                        if mj == 0 and gi == 0:
                            v.wait_ge(cV.h, cv_cst)
                        v.wait_ge(cT.h, ct_g)
                        last = None
                        for kk in range(4):
                            ki = gi * 4 + kk
                            bank = p[4 + kk]
                            dst = (ki * NMU + mj) * 128
                            last = v.tensor_scalar(winT[:, dst:dst + 128],
                                                   bank[0:128, 0:128],
                                                   wn_sb[:, ki:ki + 1], None, ALU.mult)
                        n = cV.inc(last)
                        grp_evac_v[mj * 2 + gi] = n
                    B["vector"](vA)
                strip_done_v[mj] = cV.n

            w1cv = [sb(f"w1cv{i}", [128, DFF], BF16, p1) for i in range(2)]
            w1_store_done = [0, 0]

            def emit_w1cv(u):
                e, ki = u // NKD, u % NKD

                def gCV(g, e=e, ki=ki, u=u):
                    if u >= 2:
                        g.wait_ge(sCW[u % 2].h, w1_store_done[u % 2])
                    sCB[u % 2].inc(g.dma_start(out=w1cv[u % 2][:],
                                               in_=w1_h[e, ki * 128:(ki + 1) * 128, :]),
                                   16)
                B["gpsimd"](gCV)
                qcb = sCB[u % 2].n

                def sCV(s, e=e, ki=ki, u=u, qcb=qcb):
                    s.wait_ge(sCB[u % 2].h, qcb)
                    sCW[u % 2].inc(s.dma_start(
                        out=w1b_s[e, :, ki * DFF:(ki + 1) * DFF],
                        in_=w1cv[u % 2][:]), 16)
                B["scalar"](sCV)
                w1_store_done[u % 2] = sCW[u % 2].n

            relx_v = [0] * NT
            rows_evac_v = [0] * NT
            for i in range(NT):
                def syB(sy, i=i):
                    if i >= NXB:
                        sy.wait_ge(cV.h, relx_v[i - NXB])
                    sX[i % NXB].inc(sy.dma_start(out=x_tl[i % NXB][:],
                                    in_=x_h[i * 128:(i + 1) * 128, :]), 16)
                B["sync"](syB)
                qs_here = sX[i % NXB].n

                def vB1(v, i=i, qs_here=qs_here):
                    xt = x_tl[i % NXB]
                    v.wait_ge(sX[i % NXB].h, qs_here)
                    v.reduce_sum(stt[:, 0:1], xt[:], AX.X)
                    v.tensor_tensor(sq_tl[:], xt[:], xt[:], ALU.mult)
                    v.reduce_sum(stt[:, 1:2], sq_tl[:], AX.X)
                    v.tensor_single_scalar(pk(i, 0), stt[:, 0:1], -1.0 / D, ALU.mult)
                    v.tensor_single_scalar(stt[:, 2:3], stt[:, 0:1], 1.0 / D, ALU.mult)
                    v.drain()
                    v.tensor_tensor(stt[:, 3:4], stt[:, 2:3], stt[:, 2:3], ALU.mult)
                    v.tensor_single_scalar(stt[:, 4:5], stt[:, 1:2], 1.0 / D, ALU.mult)
                    v.drain()
                    v.tensor_tensor(stt[:, 4:5], stt[:, 4:5], stt[:, 3:4], ALU.subtract)
                    v.drain()
                    v.tensor_single_scalar(stt[:, 4:5], stt[:, 4:5], EPS, ALU.add)
                    v.drain()
                    v.tensor_single_scalar(stt[:, 5:6], stt[:, 4:5], 1.0 + EPS,
                                           ALU.mult)
                    v.drain()
                    v.tensor_single_scalar(stt[:, 5:6], stt[:, 5:6], -EPS, ALU.add)
                    v.drain()
                    n = cV.inc(v.reduce_sum(stt[:, 6:7], xt[:, 0:2], AX.X))
                    relx_v[i] = n
                B["vector"](vB1)
                cv1 = cV.n

                def sB1(s, i=i, cv1=cv1):
                    s.wait_ge(cV.h, cv1)
                    s.activation(stt[:, 7:9], stt[:, 4:6], AF.Ln)
                    s.drain()
                    cS.inc(s.activation(packed[:, 4 * i + 1:4 * i + 3],
                                        stt[:, 7:9], AF.Exp, scale=-0.5))
                B["scalar"](sB1)
                cs1 = cS.n

                def vB2(v, i=i, cs1=cs1):
                    v.wait_ge(cS.h, cs1)
                    v.tensor_single_scalar(stt[:, 9:10], pk(i, 0), 2.0, ALU.mult)
                    v.drain()
                    v.tensor_tensor(stt[:, 9:10], stt[:, 6:7], stt[:, 9:10], ALU.add)
                    v.drain()
                    cV.inc(v.tensor_tensor(stt[:, 9:10], stt[:, 9:10], pk(i, 1),
                                           ALU.mult))
                B["vector"](vB2)
                cv2 = cV.n

                def sB2(s, cv2=cv2):
                    s.wait_ge(cV.h, cv2)
                    s.activation(ic_t[:], stt[:, 9:10], AF.Copy)
                    s.drain()
                    cS.inc(s.activation(stt[:, 10:11], ic_t[:], AF.Copy))
                B["scalar"](sB2)
                cs2 = cS.n

                def vB3(v, cs2=cs2):
                    v.wait_ge(cS.h, cs2)
                    f = stt[:, 10:11]
                    vr = stt[:, 9:10]
                    v.tensor_tensor(stt[:, 11:12], f, vr, ALU.is_gt)
                    v.drain()
                    v.tensor_tensor(f, f, stt[:, 11:12], ALU.subtract)
                    v.tensor_single_scalar(stt[:, 11:12], vr, 0.0, ALU.is_lt)
                    v.drain()
                    v.tensor_tensor(stt[:, 2:3], vr, f, ALU.not_equal)
                    v.drain()
                    v.tensor_tensor(stt[:, 11:12], stt[:, 11:12], stt[:, 2:3], ALU.mult)
                    v.drain()
                    v.tensor_tensor(f, f, stt[:, 11:12], ALU.add)
                    v.drain()
                    cV.inc(v.tensor_single_scalar(stt[:, 2:3], f, 0.125, ALU.mult))
                B["vector"](vB3)
                cv3 = cV.n

                def sB3(s, cv3=cv3):
                    s.wait_ge(cV.h, cv3)
                    s.activation(ic_t[:], stt[:, 2:3], AF.Copy)
                    s.drain()
                    cS.inc(s.activation(stt[:, 3:4], ic_t[:], AF.Copy))
                B["scalar"](sB3)
                cs3 = cS.n

                def vB4(v, i=i, cs3=cs3):
                    v.wait_ge(cS.h, cs3)
                    f2 = stt[:, 3:4]
                    u8 = stt[:, 2:3]
                    tr = stt[:, 10:11]
                    v.tensor_tensor(stt[:, 11:12], f2, u8, ALU.is_gt)
                    v.drain()
                    v.tensor_tensor(f2, f2, stt[:, 11:12], ALU.subtract)
                    v.drain()
                    v.tensor_single_scalar(f2, f2, -8.0, ALU.mult)
                    v.drain()
                    cV.inc(v.tensor_tensor(pk(i, 3), tr, f2, ALU.add))
                B["vector"](vB4)
                cv4 = cV.n

                def tB(t, i=i, cv4=cv4):
                    t.wait_ge(cV.h, cv4)
                    if i == 0:
                        t.wait_ge(qC.h, qc_all)
                    if i > 0:
                        t.wait_ge(cV.h, rows_evac_v[i - 1])
                    cT.inc(t.transpose(p[0][0:4, 0:128],
                                       packed[:, 4 * i:4 * i + 4], eye_f[:]))
                B["tensor"](tB)
                ct_b = cT.n

                def vB5(v, i=i, ct_b=ct_b):
                    v.wait_ge(cT.h, ct_b)
                    n = cV.inc(v.tensor_copy(rows4[0:4, i * 128:(i + 1) * 128],
                                             p[0][0:4, 0:128]))
                    rows_evac_v[i] = n
                B["vector"](vB5)
                for uu in range(4 * i, 4 * i + 4):
                    emit_w1cv(uu)

            w1cv_final = (sCW[0].n, sCW[1].n)

            def syC(sy):
                sy.wait_ge(cV.h, rows_evac_v[NT - 1])
                sRC.inc(sy.dma_start(out=r_nmean[:], in_=rows4[0:1, :]), 16)
                sRC.inc(sy.dma_start(out=r_rstd[:], in_=rows4[1:2, :]), 16)
                sRC.inc(sy.dma_start(out=r_rstdh[:], in_=rows4[2:3, :]), 16)
                sRC.inc(sy.dma_start(out=r_route[:], in_=rows4[3:4, :]), 16)
            B["sync"](syC)
            qs_rows = sRC.n

            items = [("rowa", None), ("rowb", None)] + [("mask", e) for e in range(NE)]
            tmp_free_qs = [0, 0]
            for k, (kind, e) in enumerate(items):
                def vCk(v, k=k, kind=kind, e=e):
                    if k < 2:
                        v.wait_ge(sRC.h, qs_rows)
                    if tmp_free_qs[k % 2]:
                        v.wait_ge(sRT[k % 2].h, tmp_free_qs[k % 2])
                    dst = rowt[k % 2]
                    if kind == "rowa":
                        cV.inc(v.tensor_single_scalar(dst[:], r_rstd[:], 1.0, ALU.add))
                    elif kind == "rowb":
                        cV.inc(v.tensor_tensor(dst[:], r_nmean[:], r_rstd[:], ALU.mult))
                    else:
                        v.tensor_single_scalar(dst[:], r_route[:], float(e),
                                               ALU.is_equal)
                        v.tensor_single_scalar(rowt2[:], r_route[:],
                                               float((e - 1) % NE), ALU.is_equal)
                        v.tensor_tensor(dst[:], dst[:], rowt2[:], ALU.add)
                        cV.inc(v.tensor_single_scalar(dst[:], dst[:], 0.5, ALU.mult))
                B["vector"](vCk)
                cv_k = cV.n

                def syCk(sy, k=k, kind=kind, e=e, cv_k=cv_k):
                    sy.wait_ge(cV.h, cv_k)
                    dst = (rows_s[0:1, :] if kind == "rowa" else
                           rows_s[1:2, :] if kind == "rowb" else
                           mask_s[e:e + 1, :])
                    sRT[k % 2].inc(sy.dma_start(out=dst, in_=rowt[k % 2][:]), 16)
                B["sync"](syCk)
                tmp_free_qs[k % 2] = sRT[k % 2].n

            if debug:
                def syDbg(sy):
                    sy.wait_ge(sRT[0].h, sRT[0].n)
                    sy.wait_ge(sRT[1].h, sRT[1].n)
                    sRC.inc(sy.dma_start(out=dbg_pk, in_=packed[:]), 16)
                    sRC.inc(sy.dma_start(out=dbg_rows[0:1, :], in_=r_nmean[:]), 16)
                    sRC.inc(sy.dma_start(out=dbg_rows[1:2, :], in_=r_rstd[:]), 16)
                    sRC.inc(sy.dma_start(out=dbg_rows[2:3, :], in_=r_rstdh[:]), 16)
                    sRC.inc(sy.dma_start(out=dbg_rows[3:4, :], in_=r_route[:]), 16)
                B["sync"](syDbg)
        qs_masks_rt = (sRT[0].n, sRT[1].n)

        barrier()
        if upto <= 1:
            finish()
            p12.__exit__(None, None, None)
            return nc

        p2 = contextlib.ExitStack()
        with p2:
            x_st = sb("x_st", [128, 4 * D], F32, p2)
            xT = sb("xT", [128, NKD * ST], F32, p2)
            bA = [sb(f"bA{i}", [128, ST], F32, p2) for i in range(3)]
            tb = sb("tb", [128, ST], F32, p2)
            xnb_st = sb("xnb_st", [128, NKD * ST], BF16, p2)
            hb_st = sb("hb_st", [128, NKD * ST], BF16, p2)
            ubf_st = sb("ubf_st", [128, NKI * ST], BF16, p2)
            ufh_st = sb("ufh_st", [128, 8 * ST], F32, p2)
            szb_st = sb("szb_st", [128, NKI * ST], BF16, p2)

            x_rel_t = 0
            norm_done_qs = 0
            stage_done_qs = 0
            ufh_done_qs = 0

            for s in range(NST):
                def sy0(sy, s=s, xrel=x_rel_t):
                    if s > 0:
                        sy.wait_ge(cT.h, xrel)
                    sXS.inc(sy.dma_start(
                        out=x_st[:].rearrange("q (j d) -> q j d", j=4),
                        in_=x_h[s * ST:(s + 1) * ST, :].rearrange(
                            "(j q) d -> q j d", q=128)), 16)
                B["sync"](sy0)
                qs_x = sXS.n

                xg_v = [0] * 8
                for g in range(8):
                    j, half = g // 2, g % 2
                    def tX(t, j=j, half=half, g=g, qs_x=qs_x):
                        if g == 0:
                            t.wait_ge(sXS.h, qs_x)
                        if g >= 1:
                            t.wait_ge(cV.h, xg_v[g - 1])
                        mi = None
                        for kk in range(4):
                            kd = half * 4 + kk
                            mi = t.transpose(p[kk][0:128, 0:128],
                                             x_st[:, j * D + kd * 128: j * D + (kd + 1) * 128],
                                             eye_f[:])
                        cT.inc(mi)
                    B["tensor"](tX)
                    ct_g = cT.n

                    def vX(v, j=j, half=half, g=g, ct_g=ct_g):
                        v.wait_ge(cT.h, ct_g)
                        last = None
                        for kk in range(4):
                            kd = half * 4 + kk
                            last = v.tensor_copy(
                                xT[:, kd * ST + j * 128: kd * ST + (j + 1) * 128],
                                p[kk][0:128, 0:128])
                        n = cV.inc(last)
                        xg_v[g] = n
                    B["vector"](vX)
                x_rel_t = cT.n

                def tBc(t, s=s, last_xe=cV.n):
                    t.wait_ge(cV.h, last_xe)
                    for r_i, row in enumerate((r_nmean, r_rstd, r_rstdh)):
                        cT.inc(t.matmul(p[4 + r_i][:], ones_f[:],
                                        row[:, s * ST:(s + 1) * ST],
                                        start=True, stop=True))
                B["tensor"](tBc)
                ct_bc = cT.n

                def vN(v, ct_bc=ct_bc, ndq=norm_done_qs):
                    v.wait_ge(cT.h, ct_bc)
                    if ndq:
                        v.wait_ge(sXN.h, ndq)
                    for r_i in range(3):
                        v.tensor_copy(bA[r_i][:], p[4 + r_i][:])
                    last = None
                    for kd in range(NKD):
                        sl = slice(kd * ST, (kd + 1) * ST)
                        v.tensor_tensor(tb[:], xT[:, sl], bA[0][:], ALU.add)
                        v.tensor_tensor(xnb_st[:, sl], tb[:], bA[1][:], ALU.mult)
                        last = v.tensor_tensor(hb_st[:, sl], tb[:], bA[2][:], ALU.mult)
                    cV.inc(last)
                B["vector"](vN)
                cv_norm = cV.n

                def syN(sy, s=s, cv_norm=cv_norm):
                    sy.wait_ge(cV.h, cv_norm)
                    sXN.inc(sy.dma_start(out=xnb_s[s], in_=xnb_st[:]), 16)
                B["sync"](syN)
                norm_done_qs = sXN.n

                ev_s = [0] * NMU
                for mj in range(NMU):
                    def tU(t, mj=mj, cv_norm=cv_norm):
                        if mj == 0:
                            t.wait_ge(cV.h, cv_norm)
                        if mj >= 2:
                            t.wait_ge(cS.h, ev_s[mj - 2])
                        mi = None
                        for ki in range(NKD):
                            mi = t.matmul(p[mj % 2][:],
                                          winT[:, (ki * NMU + mj) * 128:(ki * NMU + mj + 1) * 128],
                                          hb_st[:, ki * ST:(ki + 1) * ST],
                                          start=(ki == 0), stop=(ki == NKD - 1))
                        cT.inc(mi)
                    B["tensor"](tU)
                    ct_u = cT.n

                    def sU(s_, mj=mj, ct_u=ct_u, sdq=stage_done_qs,
                           hdq=ufh_done_qs):
                        s_.wait_ge(cT.h, ct_u)
                        if mj == 0 and sdq:
                            s_.wait_ge(sSG.h, sdq)
                        if mj == 0 and hdq:
                            s_.wait_ge(sUF.h, hdq)
                        if mj == 8 and hdq:
                            s_.wait_ge(sUF.h, hdq)
                        bank = p[mj % 2]
                        if mj < NMJ:
                            sl = slice(mj * ST, (mj + 1) * ST)
                            hsl = slice((mj % 8) * ST, (mj % 8 + 1) * ST)
                            s_.activation(ubf_st[:, sl], bank[:], AF.Silu)
                            n = cS.inc(s_.activation(ufh_st[:, hsl], bank[:], AF.Silu))
                        else:
                            sl = slice((mj - NMJ) * ST, (mj - NMJ + 1) * ST)
                            n = cS.inc(s_.activation(szb_st[:, sl], bank[:], AF.Silu))
                        ev_s[mj] = n
                    B["scalar"](sU)

                    if mj == 7:
                        def syH1(sy, s=s, tgt=cS.n):
                            sy.wait_ge(cS.h, tgt)
                            sUF.inc(sy.dma_start(out=uf_s[s, :, 0:8 * ST],
                                                 in_=ufh_st[:]), 16)
                        B["sync"](syH1)
                        ufh_done_qs = sUF.n
                    elif mj == 15:
                        def syH2(sy, s=s, tgt=cS.n):
                            sy.wait_ge(cS.h, tgt)
                            sUF.inc(sy.dma_start(out=uf_s[s, :, 8 * ST:16 * ST],
                                                 in_=ufh_st[:]), 16)
                        B["sync"](syH2)
                        ufh_done_qs = sUF.n

                def syU(sy, s=s):
                    sy.wait_ge(cS.h, ev_s[NMU - 1])
                    sSG.inc(sy.dma_start(out=ubf_s[s], in_=ubf_st[:]), 16)
                    sSG.inc(sy.dma_start(out=szb_s[s], in_=szb_st[:]), 16)
                B["sync"](syU)
                stage_done_qs = sSG.n

        barrier()
        p12.__exit__(None, None, None)
        if upto <= 2:
            finish()
            return nc

        p34 = contextlib.ExitStack()
        p34.__enter__()
        wdtT = sb("wdtT", [128, NKI * NMJ * 128], BF16, p34)
        woutT = sb("woutT", [128, NKI * NMD * 128], BF16, p34)
        p3 = contextlib.ExitStack()
        with p3:
            NWB3 = 2
            wst3 = [sb(f"wst3_{i}", [128, INNER], F32, p3) for i in range(NWB3)]
            jobs = ([("dt", mj) for mj in range(NMJ)] +
                    [("out", mj) for mj in range(NMD)])
            strip_rel3 = [0] * len(jobs)
            grp_ev3 = [0] * (len(jobs) * 4)
            for jn, (kind, mj) in enumerate(jobs):
                def gP3(g, jn=jn, kind=kind, mj=mj):
                    if jn >= NWB3:
                        g.wait_ge(cV.h, strip_rel3[jn - NWB3])
                    src = (wdt_h[mj * 128:(mj + 1) * 128, :] if kind == "dt"
                           else wout_h[mj * 128:(mj + 1) * 128, :])
                    sW[jn % NWB3].inc(g.dma_start(out=wst3[jn % NWB3][:], in_=src), 16)
                B["gpsimd"](gP3)
                qg_j = sW[jn % NWB3].n

                for gi in range(4):
                    def tP3(t, jn=jn, gi=gi, qg_j=qg_j):
                        if gi == 0:
                            t.wait_ge(sW[jn % NWB3].h, qg_j)
                        gidx = jn * 4 + gi
                        if gidx >= 2:
                            t.wait_ge(cV.h, grp_ev3[gidx - 2])
                        mi = None
                        for kk in range(4):
                            ki = gi * 4 + kk
                            bank = p[(gidx % 2) * 4 + kk]
                            mi = t.transpose(bank[0:128, 0:128],
                                             wst3[jn % NWB3][:, ki * 128:(ki + 1) * 128],
                                             eye_f[:])
                        cT.inc(mi)
                    B["tensor"](tP3)
                    ct_g = cT.n

                    def vP3(v, jn=jn, gi=gi, kind=kind, mj=mj, ct_g=ct_g):
                        v.wait_ge(cT.h, ct_g)
                        gidx = jn * 4 + gi
                        last = None
                        for kk in range(4):
                            ki = gi * 4 + kk
                            bank = p[(gidx % 2) * 4 + kk]
                            if kind == "dt":
                                dst = (ki * NMJ + mj) * 128
                                last = v.tensor_copy(wdtT[:, dst:dst + 128],
                                                     bank[0:128, 0:128])
                            else:
                                dst = (ki * NMD + mj) * 128
                                last = v.tensor_copy(woutT[:, dst:dst + 128],
                                                     bank[0:128, 0:128])
                        n = cV.inc(last)
                        grp_ev3[gidx] = n
                    B["vector"](vP3)
                strip_rel3[jn] = cV.n

        barrier()
        if upto <= 3:
            finish()
            p34.__exit__(None, None, None)
            return nc

        p4 = contextlib.ExitStack()
        with p4:
            ubf4 = sb("ubf4", [128, NKI * ST], BF16, p4)
            uf4 = sb("uf4", [128, NKI * ST], F32, p4)
            szb4 = sb("szb4", [128, NKI * ST], BF16, p4)
            pbf4 = sb("pbf4", [128, NKI * ST], BF16, p4)
            lam4 = [sb(f"lam4_{i}", [128, ST], F32, p4) for i in range(2)]
            bt4 = sb("bt4", [128, ST], F32, p4)
            st4 = sb("st4", [128, ST], F32, p4)
            ya4 = sb("ya4", [128, NKD * ST], F32, p4)

            ya_done_qs = 0
            prod_v_prev = 0
            qs_sg_done = sSG.n
            qs_uf_done = sUF.n
            for s in range(NST):
                def sy4(sy, s=s, pvp=prod_v_prev):
                    if s == 0:
                        sy.wait_ge(sSG.h, qs_sg_done)
                        sy.wait_ge(sUF.h, qs_uf_done)
                    if pvp:
                        sy.wait_ge(cV.h, pvp)
                    sIN4.inc(sy.dma_start(out=ubf4[:], in_=ubf_s[s]), 16)
                    sIN4.inc(sy.dma_start(out=uf4[:], in_=uf_s[s]), 16)
                    sIN4.inc(sy.dma_start(out=szb4[:], in_=szb_s[s]), 16)
                B["sync"](sy4)
                in_done_qs = sIN4.n

                prod_v = [0] * NMJ
                lam_rel_v = [0] * NMJ
                sig_s = [0] * NMJ
                for mj in range(NMJ):
                    def t4(t, mj=mj, idq=in_done_qs):
                        if mj == 0:
                            t.wait_ge(sIN4.h, idq)
                        if mj >= 2:
                            t.wait_ge(cS.h, sig_s[mj - 2])
                        mi = None
                        for ki in range(NKI):
                            mi = t.matmul(p[mj % 2][:],
                                          wdtT[:, (ki * NMJ + mj) * 128:(ki * NMJ + mj + 1) * 128],
                                          ubf4[:, ki * ST:(ki + 1) * ST],
                                          start=(ki == 0), stop=(ki == NKI - 1))
                        cT.inc(mi)
                    B["tensor"](t4)
                    ct_l = cT.n

                    def s4(s_, mj=mj, ct_l=ct_l):
                        s_.wait_ge(cT.h, ct_l)
                        if mj >= 2:
                            s_.wait_ge(cV.h, lam_rel_v[mj - 2])
                        n = cS.inc(s_.activation(lam4[mj % 2][:], p[mj % 2][:],
                                                 AF.Sigmoid,
                                                 bias=bdt_sb[:, mj:mj + 1]))
                        sig_s[mj] = n
                    B["scalar"](s4)
                    cs_sig = cS.n

                    def v4(v, mj=mj, cs_sig=cs_sig):
                        v.wait_ge(cS.h, cs_sig)
                        lam = lam4[mj % 2]
                        sl = slice(mj * ST, (mj + 1) * ST)
                        v.tensor_single_scalar(bt4[:], lam[:], -1.0, ALU.mult)
                        v.drain()
                        v.tensor_single_scalar(bt4[:], bt4[:], 1.0, ALU.add)
                        v.drain()
                        v.tensor_tensor(bt4[:], bt4[:], uf4[:, sl], ALU.mult)
                        v.drain()
                        v.tensor_tensor_scan(st4[:], lam[:], bt4[:],
                                             state[:, mj:mj + 1], ALU.mult, ALU.add)
                        n = cV.inc(v.tensor_tensor(pbf4[:, sl], st4[:], szb4[:, sl],
                                                   ALU.mult))
                        prod_v[mj] = n
                        n = cV.inc(v.tensor_copy(state[:, mj:mj + 1],
                                                 st4[:, ST - 1:ST]))
                        lam_rel_v[mj] = n
                    B["vector"](v4)

                yev_v = [0] * NMD
                for mj in range(NMD):
                    def t4o(t, mj=mj):
                        if mj == 0:
                            t.wait_ge(cV.h, prod_v[NMJ - 1])
                        if mj >= 2:
                            t.wait_ge(cV.h, yev_v[mj - 2])
                        mi = None
                        for ki in range(NKI):
                            mi = t.matmul(p[2 + mj % 2][:],
                                          woutT[:, (ki * NMD + mj) * 128:(ki * NMD + mj + 1) * 128],
                                          pbf4[:, ki * ST:(ki + 1) * ST],
                                          start=(ki == 0), stop=(ki == NKI - 1))
                        cT.inc(mi)
                    B["tensor"](t4o)
                    ct_y = cT.n

                    def v4o(v, mj=mj, ct_y=ct_y, ydq=ya_done_qs):
                        v.wait_ge(cT.h, ct_y)
                        if mj == 0 and ydq:
                            v.wait_ge(sYA.h, ydq)
                        n = cV.inc(v.tensor_copy(ya4[:, mj * ST:(mj + 1) * ST],
                                                 p[2 + mj % 2][:]))
                        yev_v[mj] = n
                    B["vector"](v4o)

                def sy4o(sy, s=s):
                    sy.wait_ge(cV.h, yev_v[NMD - 1])
                    sYA.inc(sy.dma_start(out=ya_s[s], in_=ya4[:]), 16)
                B["sync"](sy4o)
                ya_done_qs = sYA.n
                prod_v_prev = prod_v[NMJ - 1]

        barrier()
        p34.__exit__(None, None, None)
        if upto <= 4:
            finish()
            return nc

        acc = sb("acc", [128, NST * NMD * ST], F32)
        qs_ya_done = sYA.n
        p5 = contextlib.ExitStack()
        with p5:
            x5 = sb("x5", [128, 4 * D], F32, p5)
            xT5 = sb("xT5", [128, NKD * ST], F32, p5)
            ya5 = sb("ya5", [128, NKD * ST], F32, p5)
            rab = sb("rab", [128, ST], F32, p5)
            rbb = sb("rbb", [128, ST], F32, p5)
            rowa5 = sb("rowa5", [1, TOK], F32, p5)
            rowb5 = sb("rowb5", [1, TOK], F32, p5)
            masks5 = sb("masks5", [NE, TOK], F32, p5)

            def sy5pre(sy):
                sy.wait_ge(sRT[0].h, qs_masks_rt[0])
                sy.wait_ge(sRT[1].h, qs_masks_rt[1])
                sPRE.inc(sy.dma_start(out=rowa5[:], in_=rows_s[0:1, :]), 16)
                sPRE.inc(sy.dma_start(out=rowb5[:], in_=rows_s[1:2, :]), 16)
                sPRE.inc(sy.dma_start(out=masks5[:], in_=mask_s), 16)
            B["sync"](sy5pre)
            qs5pre = sPRE.n

            acc_v = 0
            for c in range(NST):
                def sy5(sy, c=c, av=acc_v):
                    if c == 0:
                        sy.wait_ge(sYA.h, qs_ya_done)
                    if av:
                        sy.wait_ge(cV.h, av)
                    sP5.inc(sy.dma_start(out=x5[:].rearrange("q (j d) -> q j d", j=4),
                                 in_=x_h[c * ST:(c + 1) * ST, :].rearrange(
                                     "(j q) d -> q j d", q=128)), 16)
                    sP5.inc(sy.dma_start(out=ya5[:], in_=ya_s[c]), 16)
                B["sync"](sy5)
                qs5 = sP5.n

                xg5_v = [0] * 8
                for g in range(8):
                    j, half = g // 2, g % 2
                    def tX5(t, j=j, half=half, g=g, qs5=qs5):
                        if g == 0:
                            t.wait_ge(sP5.h, qs5)
                        if g >= 1:
                            t.wait_ge(cV.h, xg5_v[g - 1])
                        mi = None
                        for kk in range(4):
                            kd = half * 4 + kk
                            mi = t.transpose(p[kk][0:128, 0:128],
                                             x5[:, j * D + kd * 128: j * D + (kd + 1) * 128],
                                             eye_f[:])
                        cT.inc(mi)
                    B["tensor"](tX5)
                    ct_g5 = cT.n

                    def vX5(v, j=j, half=half, g=g, ct_g5=ct_g5):
                        v.wait_ge(cT.h, ct_g5)
                        last = None
                        for kk in range(4):
                            kd = half * 4 + kk
                            last = v.tensor_copy(
                                xT5[:, kd * ST + j * 128: kd * ST + (j + 1) * 128],
                                p[kk][0:128, 0:128])
                        n = cV.inc(last)
                        xg5_v[g] = n
                    B["vector"](vX5)

                def t5b(t, c=c, last_xe=cV.n, qs5pre=qs5pre):
                    t.wait_ge(cV.h, last_xe)
                    if c == 0:
                        t.wait_ge(sPRE.h, qs5pre)
                    cT.inc(t.matmul(p[4][:], ones_f[:],
                                    rowa5[:, c * ST:(c + 1) * ST],
                                    start=True, stop=True))
                    cT.inc(t.matmul(p[5][:], ones_f[:],
                                    rowb5[:, c * ST:(c + 1) * ST],
                                    start=True, stop=True))
                B["tensor"](t5b)
                ct5b = cT.n

                def v5b(v, ct5b=ct5b):
                    v.wait_ge(cT.h, ct5b)
                    v.tensor_copy(rab[:], p[4][:])
                    cV.inc(v.tensor_copy(rbb[:], p[5][:]))
                B["vector"](v5b)
                cv5b = cV.n

                def t5m(t, c=c, cv5b=cv5b):
                    t.wait_ge(cV.h, cv5b)
                    for m in range(NMD):
                        cT.inc(t.matmul(p[m][:], b2_sb[:, m * 128:(m + 1) * 128],
                                        masks5[:, c * ST:(c + 1) * ST],
                                        start=True, stop=True))
                B["tensor"](t5m)
                ct5m = cT.n

                def v5m(v, c=c, ct5m=ct5m):
                    v.wait_ge(cT.h, ct5m)
                    for m in range(NMD):
                        col = (c * NMD + m) * ST
                        a = acc[:, col:col + ST]
                        v.tensor_tensor(a, xT5[:, m * ST:(m + 1) * ST], rab[:],
                                        ALU.mult)
                        v.tensor_tensor(a, a, rbb[:], ALU.add)
                        v.tensor_tensor(a, a, p[m][:], ALU.add)
                        cV.inc(v.tensor_tensor(a, a, ya5[:, m * ST:(m + 1) * ST],
                                               ALU.add))
                B["vector"](v5m)
                acc_v = cV.n

        barrier()
        if upto <= 5:
            finish()
            return nc

        p6 = contextlib.ExitStack()
        with p6:
            w1sb = sb("w1sb", [128, NKD * DFF], BF16, p6)
            h1 = sb("h1", [128, NMF * ST], BF16, p6)
            w2sb = [sb(f"w2sb{i}", [128, 2 * D], BF16, p6) for i in range(2)]
            xnb6 = [sb(f"xnb6_{i}", [128, NKD * ST], BF16, p6) for i in range(2)]
            mkrow = sb("mkrow", [1, TOK], BF16, p6)
            mkb = [sb(f"mkb{i}", [128, ST], BF16, p6) for i in range(2)]
            tmp6 = sb("tmp6", [128, ST], F32, p6)

            w1_rel_t = 0
            xnb_rel_t = [0, 0]
            mkrow_rel_t = [0]
            w2_rel_t = {}
            evac_all_v = 0

            for e in range(NE):
                def g6w1(g, e=e, rel=w1_rel_t):
                    if rel:
                        g.wait_ge(cT.h, rel)
                    if e == 0:
                        g.wait_ge(sCW[0].h, w1cv_final[0])
                        g.wait_ge(sCW[1].h, w1cv_final[1])
                    sW1.inc(g.dma_start(out=w1sb[:], in_=w1b_s[e]), 16)
                B["gpsimd"](g6w1)
                qg_w1 = sW1.n

                def sy6m(g, e=e, rel=mkrow_rel_t[0]):
                    if rel:
                        g.wait_ge(cT.h, rel)
                    if e == 0:
                        g.wait_ge(sRT[0].h, sRT[0].n)
                        g.wait_ge(sRT[1].h, sRT[1].n)
                    sMK.inc(g.dma_start(out=mkrow[:], in_=mask_s[e:e + 1, :]), 16)
                B["gpsimd"](sy6m)
                qs_mrow = sMK.n

                for c in range(NST):
                    u = e * NST + c

                    def sy6(sy, c=c, u=u, rel=xnb_rel_t[u % 2]):
                        if rel:
                            sy.wait_ge(cT.h, rel)
                        if u < 2:
                            sy.wait_ge(sXN.h, sXN.n)
                        sXB[u % 2].inc(sy.dma_start(out=xnb6[u % 2][:],
                                                    in_=xnb_s[c]), 16)
                    B["sync"](sy6)
                    qs_xnb = sXB[u % 2].n

                    def t6m(t, e=e, c=c, eav=evac_all_v, qs_mrow=qs_mrow):
                        if eav:
                            t.wait_ge(cV.h, eav)
                        if c == 0:
                            t.wait_ge(sMK.h, qs_mrow)
                        cT.inc(t.matmul(p[2][:], ones_b[:],
                                        mkrow[:, c * ST:(c + 1) * ST],
                                        start=True, stop=True))
                    B["tensor"](t6m)
                    ct6m = cT.n
                    if c == NST - 1:
                        mkrow_rel_t[0] = cT.n

                    def v6m(v, u=u, ct6m=ct6m):
                        v.wait_ge(cT.h, ct6m)
                        cV.inc(v.tensor_copy(mkb[u % 2][:], p[2][:]))
                    B["vector"](v6m)

                    h1_s = [0] * NMF
                    for mj in range(NMF):
                        def t6p1(t, u=u, mj=mj, qg_w1=qg_w1, qs_xnb=qs_xnb):
                            if mj == 0:
                                t.wait_ge(sW1.h, qg_w1)
                                t.wait_ge(sXB[u % 2].h, qs_xnb)
                            if mj >= 2:
                                t.wait_ge(cS.h, h1_s[mj - 2])
                            mi = None
                            for ki in range(NKD):
                                mi = t.matmul(p[mj % 2][:],
                                              w1sb[:, ki * DFF + mj * 128: ki * DFF + (mj + 1) * 128],
                                              xnb6[u % 2][:, ki * ST:(ki + 1) * ST],
                                              start=(ki == 0), stop=(ki == NKD - 1))
                            cT.inc(mi)
                        B["tensor"](t6p1)
                        ct_p1 = cT.n

                        def s6(s_, e=e, mj=mj, ct_p1=ct_p1):
                            s_.wait_ge(cT.h, ct_p1)
                            col = e * NMF + mj
                            n = cS.inc(s_.activation(h1[:, mj * ST:(mj + 1) * ST],
                                                     p[mj % 2][:], AF.Relu,
                                                     bias=b1_sb[:, col:col + 1]))
                            h1_s[mj] = n
                        B["scalar"](s6)
                    if c == NST - 1:
                        w1_rel_t = cT.n
                    xnb_rel_t[u % 2] = cT.n

                    for kg in range(16):
                        def g6w2(g, e=e, u=u, kg=kg):
                            key = (u - 1, 14 + kg) if kg < 2 else (u, kg - 2)
                            if key in w2_rel_t:
                                g.wait_ge(cT.h, w2_rel_t[key])
                            sW2[kg % 2].inc(g.dma_start(
                                out=w2sb[kg % 2][:].rearrange("q (kc d) -> q kc d", kc=2),
                                in_=w2_h[e, kg * 256:(kg + 1) * 256, :].rearrange(
                                    "(kc q) d -> q kc d", q=128)), 16)
                        B["gpsimd"](g6w2)
                        qg_w2 = sW2[kg % 2].n

                        def t6p2(t, kg=kg, qg_w2=qg_w2, cs_h1=cS.n, cv_now=cV.n):
                            t.wait_ge(sW2[kg % 2].h, qg_w2)
                            if kg == 0:
                                t.wait_ge(cS.h, cs_h1)
                                t.wait_ge(cV.h, cv_now)
                            mi = None
                            for kc in range(2):
                                kt = kg * 2 + kc
                                for m in range(NMD):
                                    mi = t.matmul(p[m][:],
                                                  w2sb[kg % 2][:, kc * D + m * 128: kc * D + (m + 1) * 128],
                                                  h1[:, kt * ST:(kt + 1) * ST],
                                                  start=(kt == 0), stop=(kt == 31))
                            cT.inc(mi)
                        B["tensor"](t6p2)
                        w2_rel_t[(u, kg)] = cT.n
                    ct_p2 = cT.n

                    def v6e(v, c=c, u=u, ct_p2=ct_p2):
                        v.wait_ge(cT.h, ct_p2)
                        last = None
                        for m in range(NMD):
                            col = (c * NMD + m) * ST
                            v.tensor_tensor(tmp6[:], p[m][:], mkb[u % 2][:], ALU.mult)
                            last = v.tensor_tensor(acc[:, col:col + ST],
                                                   acc[:, col:col + ST],
                                                   tmp6[:], ALU.add)
                        cV.inc(last)
                    B["vector"](v6e)
                    evac_all_v = cV.n

        barrier()
        if upto <= 6:
            finish()
            return nc

        def syOut(sy):
            for ti in range(NT):
                qO[ti % 2].inc(sy.dma_start(
                    out=out_h[:, ti * 1024:(ti + 1) * 1024],
                    in_=acc[:, ti * 1024:(ti + 1) * 1024]), 16)
            sy.wait_ge(qO[0].h, qO[0].n)
            sy.wait_ge(qO[1].h, qO[1].n)
            for sm in (qC, qE, sXS, sXN, sUF, sSG, sRC, sIN4, sYA, sP5,
                       sPRE, sW1, sMK):
                sy.wait_ge(sm.h, sm.n)
            for sm in sX + sW + sRT + sXB + sW2 + sCB + sCW:
                sy.wait_ge(sm.h, sm.n)
        B["sync"](syOut)
        barrier()

        return nc

        p2 = contextlib.ExitStack()
        with p2:
            x_st = sb("x_st", [128, 4 * D], F32, p2)
            xT = sb("xT", [128, NKD * ST], F32, p2)
            bA = [sb(f"bA{i}", [128, ST], F32, p2) for i in range(3)]
            tb = sb("tb", [128, ST], F32, p2)
            xnb_st = sb("xnb_st", [128, NKD * ST], BF16, p2)
            hb_st = sb("hb_st", [128, NKD * ST], BF16, p2)
            ubf_st = sb("ubf_st", [128, NKI * ST], BF16, p2)
            ufh_st = sb("ufh_st", [128, 8 * ST], F32, p2)
            szb_st = sb("szb_st", [128, NKI * ST], BF16, p2)

            x_rel_t = 0
            norm_done_qs = 0
            stage_done_qs = 0
            ufh_done_qs = 0

            for s in range(NST):
                def sy0(sy, s=s, xrel=x_rel_t):
                    if s > 0:
                        sy.wait_ge(cT.h, xrel)
                    sXS.inc(sy.dma_start(
                        out=x_st[:].rearrange("q (j d) -> q j d", j=4),
                        in_=x_h[s * ST:(s + 1) * ST, :].rearrange(
                            "(j q) d -> q j d", q=128)), 16)
                B["sync"](sy0)
                qs_x = sXS.n

                xg_v = [0] * 8
                for g in range(8):
                    j, half = g // 2, g % 2
                    def tX(t, j=j, half=half, g=g, qs_x=qs_x):
                        if g == 0:
                            t.wait_ge(sXS.h, qs_x)
                        if g >= 1:
                            t.wait_ge(cV.h, xg_v[g - 1])
                        mi = None
                        for kk in range(4):
                            kd = half * 4 + kk
                            mi = t.transpose(p[kk][0:128, 0:128],
                                             x_st[:, j * D + kd * 128: j * D + (kd + 1) * 128],
                                             eye_f[:])
                        cT.inc(mi)
                    B["tensor"](tX)
                    ct_g = cT.n

                    def vX(v, j=j, half=half, g=g, ct_g=ct_g):
                        v.wait_ge(cT.h, ct_g)
                        last = None
                        for kk in range(4):
                            kd = half * 4 + kk
                            last = v.tensor_copy(
                                xT[:, kd * ST + j * 128: kd * ST + (j + 1) * 128],
                                p[kk][0:128, 0:128])
                        n = cV.inc(last)
                        xg_v[g] = n
                    B["vector"](vX)
                x_rel_t = cT.n

                def tBc(t, s=s, last_xe=cV.n):
                    t.wait_ge(cV.h, last_xe)
                    for r_i, row in enumerate((r_nmean, r_rstd, r_rstdh)):
                        cT.inc(t.matmul(p[4 + r_i][:], ones_f[:],
                                        row[:, s * ST:(s + 1) * ST],
                                        start=True, stop=True))
                B["tensor"](tBc)
                ct_bc = cT.n

                def vN(v, ct_bc=ct_bc, ndq=norm_done_qs):
                    v.wait_ge(cT.h, ct_bc)
                    if ndq:
                        v.wait_ge(sXN.h, ndq)
                    for r_i in range(3):
                        v.tensor_copy(bA[r_i][:], p[4 + r_i][:])
                    last = None
                    for kd in range(NKD):
                        sl = slice(kd * ST, (kd + 1) * ST)
                        v.tensor_tensor(tb[:], xT[:, sl], bA[0][:], ALU.add)
                        v.tensor_tensor(xnb_st[:, sl], tb[:], bA[1][:], ALU.mult)
                        last = v.tensor_tensor(hb_st[:, sl], tb[:], bA[2][:], ALU.mult)
                    cV.inc(last)
                B["vector"](vN)
                cv_norm = cV.n

                def syN(sy, s=s, cv_norm=cv_norm):
                    sy.wait_ge(cV.h, cv_norm)
                    sXN.inc(sy.dma_start(out=xnb_s[s], in_=xnb_st[:]), 16)
                B["sync"](syN)
                norm_done_qs = sXN.n

                ev_s = [0] * NMU
                for mj in range(NMU):
                    def tU(t, mj=mj, cv_norm=cv_norm):
                        if mj == 0:
                            t.wait_ge(cV.h, cv_norm)
                        if mj >= 2:
                            t.wait_ge(cS.h, ev_s[mj - 2])
                        mi = None
                        for ki in range(NKD):
                            mi = t.matmul(p[mj % 2][:],
                                          winT[:, (ki * NMU + mj) * 128:(ki * NMU + mj + 1) * 128],
                                          hb_st[:, ki * ST:(ki + 1) * ST],
                                          start=(ki == 0), stop=(ki == NKD - 1))
                        cT.inc(mi)
                    B["tensor"](tU)
                    ct_u = cT.n

                    def sU(s_, mj=mj, ct_u=ct_u, sdq=stage_done_qs,
                           hdq=ufh_done_qs):
                        s_.wait_ge(cT.h, ct_u)
                        if mj == 0 and sdq:
                            s_.wait_ge(sSG.h, sdq)
                        if mj == 0 and hdq:
                            s_.wait_ge(sUF.h, hdq)
                        if mj == 8 and hdq:
                            s_.wait_ge(sUF.h, hdq)
                        bank = p[mj % 2]
                        if mj < NMJ:
                            sl = slice(mj * ST, (mj + 1) * ST)
                            hsl = slice((mj % 8) * ST, (mj % 8 + 1) * ST)
                            s_.activation(ubf_st[:, sl], bank[:], AF.Silu)
                            n = cS.inc(s_.activation(ufh_st[:, hsl], bank[:], AF.Silu))
                        else:
                            sl = slice((mj - NMJ) * ST, (mj - NMJ + 1) * ST)
                            n = cS.inc(s_.activation(szb_st[:, sl], bank[:], AF.Silu))
                        ev_s[mj] = n
                    B["scalar"](sU)

                    if mj == 7:
                        def syH1(sy, s=s, tgt=cS.n):
                            sy.wait_ge(cS.h, tgt)
                            sUF.inc(sy.dma_start(out=uf_s[s, :, 0:8 * ST],
                                                 in_=ufh_st[:]), 16)
                        B["sync"](syH1)
                        ufh_done_qs = sUF.n
                    elif mj == 15:
                        def syH2(sy, s=s, tgt=cS.n):
                            sy.wait_ge(cS.h, tgt)
                            sUF.inc(sy.dma_start(out=uf_s[s, :, 8 * ST:16 * ST],
                                                 in_=ufh_st[:]), 16)
                        B["sync"](syH2)
                        ufh_done_qs = sUF.n

                def syU(sy, s=s):
                    sy.wait_ge(cS.h, ev_s[NMU - 1])
                    sSG.inc(sy.dma_start(out=ubf_s[s], in_=ubf_st[:]), 16)
                    sSG.inc(sy.dma_start(out=szb_s[s], in_=szb_st[:]), 16)
                B["sync"](syU)
                stage_done_qs = sSG.n

        barrier()
        p12.__exit__(None, None, None)
        if upto <= 2:
            finish()
            return nc

        p34 = contextlib.ExitStack()
        p34.__enter__()
        wdtT = sb("wdtT", [128, NKI * NMJ * 128], BF16, p34)
        woutT = sb("woutT", [128, NKI * NMD * 128], BF16, p34)
        p3 = contextlib.ExitStack()
        with p3:
            NWB3 = 2
            wst3 = [sb(f"wst3_{i}", [128, INNER], F32, p3) for i in range(NWB3)]
            jobs = ([("dt", mj) for mj in range(NMJ)] +
                    [("out", mj) for mj in range(NMD)])
            strip_rel3 = [0] * len(jobs)
            grp_ev3 = [0] * (len(jobs) * 4)
            for jn, (kind, mj) in enumerate(jobs):
                def gP3(g, jn=jn, kind=kind, mj=mj):
                    if jn >= NWB3:
                        g.wait_ge(cV.h, strip_rel3[jn - NWB3])
                    src = (wdt_h[mj * 128:(mj + 1) * 128, :] if kind == "dt"
                           else wout_h[mj * 128:(mj + 1) * 128, :])
                    sW[jn % NWB3].inc(g.dma_start(out=wst3[jn % NWB3][:], in_=src), 16)
                B["gpsimd"](gP3)
                qg_j = sW[jn % NWB3].n

                for gi in range(4):
                    def tP3(t, jn=jn, gi=gi, qg_j=qg_j):
                        if gi == 0:
                            t.wait_ge(sW[jn % NWB3].h, qg_j)
                        gidx = jn * 4 + gi
                        if gidx >= 2:
                            t.wait_ge(cV.h, grp_ev3[gidx - 2])
                        mi = None
                        for kk in range(4):
                            ki = gi * 4 + kk
                            bank = p[(gidx % 2) * 4 + kk]
                            mi = t.transpose(bank[0:128, 0:128],
                                             wst3[jn % NWB3][:, ki * 128:(ki + 1) * 128],
                                             eye_f[:])
                        cT.inc(mi)
                    B["tensor"](tP3)
                    ct_g = cT.n

                    def vP3(v, jn=jn, gi=gi, kind=kind, mj=mj, ct_g=ct_g):
                        v.wait_ge(cT.h, ct_g)
                        gidx = jn * 4 + gi
                        last = None
                        for kk in range(4):
                            ki = gi * 4 + kk
                            bank = p[(gidx % 2) * 4 + kk]
                            if kind == "dt":
                                dst = (ki * NMJ + mj) * 128
                                last = v.tensor_copy(wdtT[:, dst:dst + 128],
                                                     bank[0:128, 0:128])
                            else:
                                dst = (ki * NMD + mj) * 128
                                last = v.tensor_copy(woutT[:, dst:dst + 128],
                                                     bank[0:128, 0:128])
                        n = cV.inc(last)
                        grp_ev3[gidx] = n
                    B["vector"](vP3)
                strip_rel3[jn] = cV.n

        barrier()
        if upto <= 3:
            finish()
            p34.__exit__(None, None, None)
            return nc

        p4 = contextlib.ExitStack()
        with p4:
            ubf4 = sb("ubf4", [128, NKI * ST], BF16, p4)
            uf4 = sb("uf4", [128, NKI * ST], F32, p4)
            szb4 = sb("szb4", [128, NKI * ST], BF16, p4)
            pbf4 = sb("pbf4", [128, NKI * ST], BF16, p4)
            lam4 = [sb(f"lam4_{i}", [128, ST], F32, p4) for i in range(2)]
            bt4 = sb("bt4", [128, ST], F32, p4)
            st4 = sb("st4", [128, ST], F32, p4)
            ya4 = sb("ya4", [128, NKD * ST], F32, p4)

            ya_done_qs = 0
            prod_v_prev = 0
            qs_sg_done = sSG.n
            qs_uf_done = sUF.n
            for s in range(NST):
                def sy4(sy, s=s, pvp=prod_v_prev):
                    if s == 0:
                        sy.wait_ge(sSG.h, qs_sg_done)
                        sy.wait_ge(sUF.h, qs_uf_done)
                    if pvp:
                        sy.wait_ge(cV.h, pvp)
                    sIN4.inc(sy.dma_start(out=ubf4[:], in_=ubf_s[s]), 16)
                    sIN4.inc(sy.dma_start(out=uf4[:], in_=uf_s[s]), 16)
                    sIN4.inc(sy.dma_start(out=szb4[:], in_=szb_s[s]), 16)
                B["sync"](sy4)
                in_done_qs = sIN4.n

                prod_v = [0] * NMJ
                lam_rel_v = [0] * NMJ
                sig_s = [0] * NMJ
                for mj in range(NMJ):
                    def t4(t, mj=mj, idq=in_done_qs):
                        if mj == 0:
                            t.wait_ge(sIN4.h, idq)
                        if mj >= 2:
                            t.wait_ge(cS.h, sig_s[mj - 2])
                        mi = None
                        for ki in range(NKI):
                            mi = t.matmul(p[mj % 2][:],
                                          wdtT[:, (ki * NMJ + mj) * 128:(ki * NMJ + mj + 1) * 128],
                                          ubf4[:, ki * ST:(ki + 1) * ST],
                                          start=(ki == 0), stop=(ki == NKI - 1))
                        cT.inc(mi)
                    B["tensor"](t4)
                    ct_l = cT.n

                    def s4(s_, mj=mj, ct_l=ct_l):
                        s_.wait_ge(cT.h, ct_l)
                        if mj >= 2:
                            s_.wait_ge(cV.h, lam_rel_v[mj - 2])
                        n = cS.inc(s_.activation(lam4[mj % 2][:], p[mj % 2][:],
                                                 AF.Sigmoid,
                                                 bias=bdt_sb[:, mj:mj + 1]))
                        sig_s[mj] = n
                    B["scalar"](s4)
                    cs_sig = cS.n

                    def v4(v, mj=mj, cs_sig=cs_sig):
                        v.wait_ge(cS.h, cs_sig)
                        lam = lam4[mj % 2]
                        sl = slice(mj * ST, (mj + 1) * ST)
                        v.tensor_single_scalar(bt4[:], lam[:], -1.0, ALU.mult)
                        v.drain()
                        v.tensor_single_scalar(bt4[:], bt4[:], 1.0, ALU.add)
                        v.drain()
                        v.tensor_tensor(bt4[:], bt4[:], uf4[:, sl], ALU.mult)
                        v.drain()
                        v.tensor_tensor_scan(st4[:], lam[:], bt4[:],
                                             state[:, mj:mj + 1], ALU.mult, ALU.add)
                        n = cV.inc(v.tensor_tensor(pbf4[:, sl], st4[:], szb4[:, sl],
                                                   ALU.mult))
                        prod_v[mj] = n
                        n = cV.inc(v.tensor_copy(state[:, mj:mj + 1],
                                                 st4[:, ST - 1:ST]))
                        lam_rel_v[mj] = n
                    B["vector"](v4)

                yev_v = [0] * NMD
                for mj in range(NMD):
                    def t4o(t, mj=mj):
                        if mj == 0:
                            t.wait_ge(cV.h, prod_v[NMJ - 1])
                        if mj >= 2:
                            t.wait_ge(cV.h, yev_v[mj - 2])
                        mi = None
                        for ki in range(NKI):
                            mi = t.matmul(p[2 + mj % 2][:],
                                          woutT[:, (ki * NMD + mj) * 128:(ki * NMD + mj + 1) * 128],
                                          pbf4[:, ki * ST:(ki + 1) * ST],
                                          start=(ki == 0), stop=(ki == NKI - 1))
                        cT.inc(mi)
                    B["tensor"](t4o)
                    ct_y = cT.n

                    def v4o(v, mj=mj, ct_y=ct_y, ydq=ya_done_qs):
                        v.wait_ge(cT.h, ct_y)
                        if mj == 0 and ydq:
                            v.wait_ge(sYA.h, ydq)
                        n = cV.inc(v.tensor_copy(ya4[:, mj * ST:(mj + 1) * ST],
                                                 p[2 + mj % 2][:]))
                        yev_v[mj] = n
                    B["vector"](v4o)

                def sy4o(sy, s=s):
                    sy.wait_ge(cV.h, yev_v[NMD - 1])
                    sYA.inc(sy.dma_start(out=ya_s[s], in_=ya4[:]), 16)
                B["sync"](sy4o)
                ya_done_qs = sYA.n
                prod_v_prev = prod_v[NMJ - 1]

        barrier()
        p34.__exit__(None, None, None)
        if upto <= 4:
            finish()
            return nc

        acc = sb("acc", [128, NST * NMD * ST], F32)
        qs_ya_done = sYA.n
        p5 = contextlib.ExitStack()
        with p5:
            x5 = sb("x5", [128, 4 * D], F32, p5)
            xT5 = sb("xT5", [128, NKD * ST], F32, p5)
            ya5 = sb("ya5", [128, NKD * ST], F32, p5)
            rab = sb("rab", [128, ST], F32, p5)
            rbb = sb("rbb", [128, ST], F32, p5)
            rowa5 = sb("rowa5", [1, TOK], F32, p5)
            rowb5 = sb("rowb5", [1, TOK], F32, p5)
            masks5 = sb("masks5", [NE, TOK], F32, p5)

            def sy5pre(sy):
                sy.wait_ge(sRT[0].h, qs_masks_rt[0])
                sy.wait_ge(sRT[1].h, qs_masks_rt[1])
                sPRE.inc(sy.dma_start(out=rowa5[:], in_=rows_s[0:1, :]), 16)
                sPRE.inc(sy.dma_start(out=rowb5[:], in_=rows_s[1:2, :]), 16)
                sPRE.inc(sy.dma_start(out=masks5[:], in_=mask_s), 16)
            B["sync"](sy5pre)
            qs5pre = sPRE.n

            acc_v = 0
            for c in range(NST):
                def sy5(sy, c=c, av=acc_v):
                    if c == 0:
                        sy.wait_ge(sYA.h, qs_ya_done)
                    if av:
                        sy.wait_ge(cV.h, av)
                    sP5.inc(sy.dma_start(out=x5[:].rearrange("q (j d) -> q j d", j=4),
                                 in_=x_h[c * ST:(c + 1) * ST, :].rearrange(
                                     "(j q) d -> q j d", q=128)), 16)
                    sP5.inc(sy.dma_start(out=ya5[:], in_=ya_s[c]), 16)
                B["sync"](sy5)
                qs5 = sP5.n

                xg5_v = [0] * 8
                for g in range(8):
                    j, half = g // 2, g % 2
                    def tX5(t, j=j, half=half, g=g, qs5=qs5):
                        if g == 0:
                            t.wait_ge(sP5.h, qs5)
                        if g >= 1:
                            t.wait_ge(cV.h, xg5_v[g - 1])
                        mi = None
                        for kk in range(4):
                            kd = half * 4 + kk
                            mi = t.transpose(p[kk][0:128, 0:128],
                                             x5[:, j * D + kd * 128: j * D + (kd + 1) * 128],
                                             eye_f[:])
                        cT.inc(mi)
                    B["tensor"](tX5)
                    ct_g5 = cT.n

                    def vX5(v, j=j, half=half, g=g, ct_g5=ct_g5):
                        v.wait_ge(cT.h, ct_g5)
                        last = None
                        for kk in range(4):
                            kd = half * 4 + kk
                            last = v.tensor_copy(
                                xT5[:, kd * ST + j * 128: kd * ST + (j + 1) * 128],
                                p[kk][0:128, 0:128])
                        n = cV.inc(last)
                        xg5_v[g] = n
                    B["vector"](vX5)

                def t5b(t, c=c, last_xe=cV.n, qs5pre=qs5pre):
                    t.wait_ge(cV.h, last_xe)
                    if c == 0:
                        t.wait_ge(sPRE.h, qs5pre)
                    cT.inc(t.matmul(p[4][:], ones_f[:],
                                    rowa5[:, c * ST:(c + 1) * ST],
                                    start=True, stop=True))
                    cT.inc(t.matmul(p[5][:], ones_f[:],
                                    rowb5[:, c * ST:(c + 1) * ST],
                                    start=True, stop=True))
                B["tensor"](t5b)
                ct5b = cT.n

                def v5b(v, ct5b=ct5b):
                    v.wait_ge(cT.h, ct5b)
                    v.tensor_copy(rab[:], p[4][:])
                    cV.inc(v.tensor_copy(rbb[:], p[5][:]))
                B["vector"](v5b)
                cv5b = cV.n

                def t5m(t, c=c, cv5b=cv5b):
                    t.wait_ge(cV.h, cv5b)
                    for m in range(NMD):
                        cT.inc(t.matmul(p[m][:], b2_sb[:, m * 128:(m + 1) * 128],
                                        masks5[:, c * ST:(c + 1) * ST],
                                        start=True, stop=True))
                B["tensor"](t5m)
                ct5m = cT.n

                def v5m(v, c=c, ct5m=ct5m):
                    v.wait_ge(cT.h, ct5m)
                    for m in range(NMD):
                        col = (c * NMD + m) * ST
                        a = acc[:, col:col + ST]
                        v.tensor_tensor(a, xT5[:, m * ST:(m + 1) * ST], rab[:],
                                        ALU.mult)
                        v.tensor_tensor(a, a, rbb[:], ALU.add)
                        v.tensor_tensor(a, a, p[m][:], ALU.add)
                        cV.inc(v.tensor_tensor(a, a, ya5[:, m * ST:(m + 1) * ST],
                                               ALU.add))
                B["vector"](v5m)
                acc_v = cV.n

        barrier()
        if upto <= 5:
            finish()
            return nc

        p6 = contextlib.ExitStack()
        with p6:
            w1sb = sb("w1sb", [128, NKD * DFF], BF16, p6)
            h1 = sb("h1", [128, NMF * ST], BF16, p6)
            w2sb = [sb(f"w2sb{i}", [128, 2 * D], BF16, p6) for i in range(2)]
            xnb6 = [sb(f"xnb6_{i}", [128, NKD * ST], BF16, p6) for i in range(2)]
            mkrow = sb("mkrow", [1, TOK], BF16, p6)
            mkb = [sb(f"mkb{i}", [128, ST], BF16, p6) for i in range(2)]
            tmp6 = sb("tmp6", [128, ST], F32, p6)

            w1_rel_t = 0
            xnb_rel_t = [0, 0]
            mkrow_rel_t = [0]
            w2_rel_t = {}
            evac_all_v = 0

            for e in range(NE):
                def g6w1(g, e=e, rel=w1_rel_t):
                    if rel:
                        g.wait_ge(cT.h, rel)
                    if e == 0:
                        g.wait_ge(sCW[0].h, w1cv_final[0])
                        g.wait_ge(sCW[1].h, w1cv_final[1])
                    sW1.inc(g.dma_start(out=w1sb[:], in_=w1b_s[e]), 16)
                B["gpsimd"](g6w1)
                qg_w1 = sW1.n

                def sy6m(g, e=e, rel=mkrow_rel_t[0]):
                    if rel:
                        g.wait_ge(cT.h, rel)
                    if e == 0:
                        g.wait_ge(sRT[0].h, sRT[0].n)
                        g.wait_ge(sRT[1].h, sRT[1].n)
                    sMK.inc(g.dma_start(out=mkrow[:], in_=mask_s[e:e + 1, :]), 16)
                B["gpsimd"](sy6m)
                qs_mrow = sMK.n

                for c in range(NST):
                    u = e * NST + c

                    def sy6(sy, c=c, u=u, rel=xnb_rel_t[u % 2]):
                        if rel:
                            sy.wait_ge(cT.h, rel)
                        if u < 2:
                            sy.wait_ge(sXN.h, sXN.n)
                        sXB[u % 2].inc(sy.dma_start(out=xnb6[u % 2][:],
                                                    in_=xnb_s[c]), 16)
                    B["sync"](sy6)
                    qs_xnb = sXB[u % 2].n

                    def t6m(t, e=e, c=c, eav=evac_all_v, qs_mrow=qs_mrow):
                        if eav:
                            t.wait_ge(cV.h, eav)
                        if c == 0:
                            t.wait_ge(sMK.h, qs_mrow)
                        cT.inc(t.matmul(p[2][:], ones_b[:],
                                        mkrow[:, c * ST:(c + 1) * ST],
                                        start=True, stop=True))
                    B["tensor"](t6m)
                    ct6m = cT.n
                    if c == NST - 1:
                        mkrow_rel_t[0] = cT.n

                    def v6m(v, u=u, ct6m=ct6m):
                        v.wait_ge(cT.h, ct6m)
                        cV.inc(v.tensor_copy(mkb[u % 2][:], p[2][:]))
                    B["vector"](v6m)

                    h1_s = [0] * NMF
                    for mj in range(NMF):
                        def t6p1(t, u=u, mj=mj, qg_w1=qg_w1, qs_xnb=qs_xnb):
                            if mj == 0:
                                t.wait_ge(sW1.h, qg_w1)
                                t.wait_ge(sXB[u % 2].h, qs_xnb)
                            if mj >= 2:
                                t.wait_ge(cS.h, h1_s[mj - 2])
                            mi = None
                            for ki in range(NKD):
                                mi = t.matmul(p[mj % 2][:],
                                              w1sb[:, ki * DFF + mj * 128: ki * DFF + (mj + 1) * 128],
                                              xnb6[u % 2][:, ki * ST:(ki + 1) * ST],
                                              start=(ki == 0), stop=(ki == NKD - 1))
                            cT.inc(mi)
                        B["tensor"](t6p1)
                        ct_p1 = cT.n

                        def s6(s_, e=e, mj=mj, ct_p1=ct_p1):
                            s_.wait_ge(cT.h, ct_p1)
                            col = e * NMF + mj
                            n = cS.inc(s_.activation(h1[:, mj * ST:(mj + 1) * ST],
                                                     p[mj % 2][:], AF.Relu,
                                                     bias=b1_sb[:, col:col + 1]))
                            h1_s[mj] = n
                        B["scalar"](s6)
                    if c == NST - 1:
                        w1_rel_t = cT.n
                    xnb_rel_t[u % 2] = cT.n

                    for kg in range(16):
                        def g6w2(g, e=e, u=u, kg=kg):
                            key = (u - 1, 14 + kg) if kg < 2 else (u, kg - 2)
                            if key in w2_rel_t:
                                g.wait_ge(cT.h, w2_rel_t[key])
                            sW2[kg % 2].inc(g.dma_start(
                                out=w2sb[kg % 2][:].rearrange("q (kc d) -> q kc d", kc=2),
                                in_=w2_h[e, kg * 256:(kg + 1) * 256, :].rearrange(
                                    "(kc q) d -> q kc d", q=128)), 16)
                        B["gpsimd"](g6w2)
                        qg_w2 = sW2[kg % 2].n

                        def t6p2(t, kg=kg, qg_w2=qg_w2, cs_h1=cS.n, cv_now=cV.n):
                            t.wait_ge(sW2[kg % 2].h, qg_w2)
                            if kg == 0:
                                t.wait_ge(cS.h, cs_h1)
                                t.wait_ge(cV.h, cv_now)
                            mi = None
                            for kc in range(2):
                                kt = kg * 2 + kc
                                for m in range(NMD):
                                    mi = t.matmul(p[m][:],
                                                  w2sb[kg % 2][:, kc * D + m * 128: kc * D + (m + 1) * 128],
                                                  h1[:, kt * ST:(kt + 1) * ST],
                                                  start=(kt == 0), stop=(kt == 31))
                            cT.inc(mi)
                        B["tensor"](t6p2)
                        w2_rel_t[(u, kg)] = cT.n
                    ct_p2 = cT.n

                    def v6e(v, c=c, u=u, ct_p2=ct_p2):
                        v.wait_ge(cT.h, ct_p2)
                        last = None
                        for m in range(NMD):
                            col = (c * NMD + m) * ST
                            v.tensor_tensor(tmp6[:], p[m][:], mkb[u % 2][:], ALU.mult)
                            last = v.tensor_tensor(acc[:, col:col + ST],
                                                   acc[:, col:col + ST],
                                                   tmp6[:], ALU.add)
                        cV.inc(last)
                    B["vector"](v6e)
                    evac_all_v = cV.n

        barrier()
        if upto <= 6:
            finish()
            return nc

        out_dst = o7_s if p7_mode == "noout" else out_h
        p7 = contextlib.ExitStack()
        with p7:
            if p7_mode == "noxpose":
                def syNX(sy):
                    for ti in range(NT):
                        qO[ti % 2].inc(sy.dma_start(
                            out=out_dst[ti * 128:(ti + 1) * 128, :],
                            in_=acc[:, ti * 1024:(ti + 1) * 1024]), 16)
                    sy.wait_ge(qO[0].h, qO[0].n)
                    sy.wait_ge(qO[1].h, qO[1].n)
                B["sync"](syNX)
                finish()
                return nc
            ostg = sb("ostg", [128, NT * D], F32, p7)
            o_gv = [0] * (NT * 2)
            for ti in range(NT):
                c, j = ti // 4, ti % 4
                for g in range(2):
                    gidx = ti * 2 + g

                    def t7(t, ti=ti, c=c, j=j, g=g, gidx=gidx):
                        if gidx >= 2:
                            t.wait_ge(cS.h, o_gv[gidx - 2])
                        mi = None
                        for kk in range(4):
                            m = g * 4 + kk
                            col = (c * NMD + m) * ST + j * 128
                            mi = t.transpose(p[(gidx % 2) * 4 + kk][0:128, 0:128],
                                             acc[:, col:col + 128], eye_f[:])
                        cT.inc(mi)
                    B["tensor"](t7)
                    ct7g = cT.n

                    def s7(s_, ti=ti, g=g, gidx=gidx, ct7g=ct7g):
                        s_.wait_ge(cT.h, ct7g)
                        last = None
                        for kk in range(4):
                            m = g * 4 + kk
                            last = s_.activation(
                                ostg[:, ti * D + m * 128: ti * D + (m + 1) * 128],
                                p[(gidx % 2) * 4 + kk][0:128, 0:128], AF.Copy)
                        n = cS.inc(last)
                        o_gv[gidx] = n
                    B["scalar"](s7)

            def syOut(sy):
                sy.wait_ge(cS.h, o_gv[NT * 2 - 1])
                for ti in range(NT):
                    qO[ti % 2].inc(sy.dma_start(
                        out=out_dst[ti * 128:(ti + 1) * 128, :],
                        in_=ostg[:, ti * D:(ti + 1) * D]), 16)
                sy.wait_ge(qO[0].h, qO[0].n)
                sy.wait_ge(qO[1].h, qO[1].n)
            B["sync"](syOut)

            def syF(sy):
                for sm in (qC, qE, sXS, sXN, sUF, sSG, sRC, sIN4, sYA, sP5,
                           sPRE, sW1, sMK):
                    sy.wait_ge(sm.h, sm.n)
                for sm in sX + sW + sRT + sXB + sW2:
                    sy.wait_ge(sm.h, sm.n)
            B["sync"](syF)
            barrier()

    return nc


def kernel(**inputs):
    nc = build(debug=False)
    x = np.ascontiguousarray(inputs["L0_attn_x"], dtype=np.float32)
    base = {
        "w_norm": np.ascontiguousarray(inputs["L0_attn_w_norm"], dtype=np.float32),
        "W_in": np.ascontiguousarray(inputs["L0_attn_W_in"], dtype=np.float32),
        "W_dt": np.ascontiguousarray(inputs["L0_attn_W_dt"], dtype=np.float32),
        "b_dt": np.ascontiguousarray(inputs["L0_attn_b_dt"], dtype=np.float32),
        "W_out": np.ascontiguousarray(inputs["L0_attn_W_out"], dtype=np.float32),
        "W1": np.ascontiguousarray(inputs["L0_ffn_W1"], dtype=np.float32),
        "b1": np.ascontiguousarray(inputs["L0_ffn_b1"], dtype=np.float32),
        "W2": np.ascontiguousarray(inputs["L0_ffn_W2"], dtype=np.float32),
        "b2": np.ascontiguousarray(inputs["L0_ffn_b2"], dtype=np.float32),
        "eye": np.eye(128, dtype=np.float32),
    }
    in_maps = [dict(base, x=np.ascontiguousarray(x[b])) for b in range(8)]
    res = run_bass_kernel_spmd(nc, in_maps, core_ids=list(range(8)))
    outs = []
    for r in res.results:
        a = np.asarray(r["out"])
        a = a.reshape(128, NST, NMD, ST)
        outs.append(np.ascontiguousarray(
            a.transpose(1, 3, 2, 0).reshape(TOK, D)))
    return np.stack(outs, axis=0)


# version 06 (final)
# speedup vs baseline: 1.1046x; 1.1046x over previous
import contextlib
import numpy as np
import concourse.bass as bass
import concourse.mybir as mybir
from concourse.bass_utils import run_bass_kernel_spmd

F32 = mybir.dt.float32
BF16 = mybir.dt.bfloat16
I32 = mybir.dt.int32
AF = mybir.ActivationFunctionType
ALU = mybir.AluOpType
AX = mybir.AxisListType

TOK = 2048
D = 1024
INNER = 2048
E2 = 4096
DFF = 4096
NE = 8
NT = TOK // 128
ST = 512
NST = TOK // ST
EPS = 1e-5
NKD = D // 128
NKI = INNER // 128
NMU = E2 // 128
NMJ = INNER // 128
NMD = D // 128
NMF = DFF // 128


class Ctr:
    def __init__(self, handle):
        self.h = handle
        self.n = 0

    def inc(self, instr, k=1):
        instr.then_inc(self.h, k)
        self.n += k
        return self.n


def build(debug=False, upto=99):
    nc = bass.Bass("TRN2", target_bir_lowering=False)
    nc.detect_race_conditions = False

    x_h = nc.dram_tensor("x", [TOK, D], F32, kind="ExternalInput").ap()
    wn_h = nc.dram_tensor("w_norm", [D], F32, kind="ExternalInput").ap()
    win_h = nc.dram_tensor("W_in", [E2, D], F32, kind="ExternalInput").ap()
    wdt_h = nc.dram_tensor("W_dt", [INNER, INNER], F32, kind="ExternalInput").ap()
    bdt_h = nc.dram_tensor("b_dt", [INNER], F32, kind="ExternalInput").ap()
    wout_h = nc.dram_tensor("W_out", [D, INNER], F32, kind="ExternalInput").ap()
    w1_h = nc.dram_tensor("W1", [NE, D, DFF], F32, kind="ExternalInput").ap()
    b1_h = nc.dram_tensor("b1", [NE, DFF], F32, kind="ExternalInput").ap()
    w2_h = nc.dram_tensor("W2", [NE, DFF, D], F32, kind="ExternalInput").ap()
    b2_h = nc.dram_tensor("b2", [NE, D], F32, kind="ExternalInput").ap()
    eye_h = nc.dram_tensor("eye", [128, 128], F32, kind="ExternalInput").ap()

    skind = "ExternalOutput" if debug else "Internal"
    ubf_s = nc.dram_tensor("s_ubf", [NST, 128, NKI * ST], BF16, kind=skind).ap()
    uf_s = nc.dram_tensor("s_uf", [NST, 128, NKI * ST], F32, kind=skind).ap()
    szb_s = nc.dram_tensor("s_szb", [NST, 128, NKI * ST], BF16, kind=skind).ap()
    xnb_s = nc.dram_tensor("s_xnb", [NST, 128, NKD * ST], BF16, kind=skind).ap()
    ya_s = nc.dram_tensor("s_ya", [NST, 128, NKD * ST], F32, kind=skind).ap()
    rows_s = nc.dram_tensor("s_rows", [2, TOK], F32, kind=skind).ap()
    out_h = nc.dram_tensor("out", [128, NST * NMD * ST], F32,
                           kind="ExternalOutput").ap()
    mask_s = nc.dram_tensor("s_mask", [NE, TOK], F32, kind=skind).ap()
    dbg_rows = (nc.dram_tensor("s_dbgrows", [8, TOK], F32, kind="ExternalOutput").ap()
                if debug else None)
    dbg_pk = (nc.dram_tensor("s_dbgpk", [128, 4 * NT], F32, kind="ExternalOutput").ap()
              if debug else None)

    gs = contextlib.ExitStack()
    with gs:
        def sb(name, shape, dt, stack=gs):
            return stack.enter_context(nc.sbuf_tensor(name, shape, dt))

        def mksem(name):
            return Ctr(gs.enter_context(nc.semaphore(name)))

        eye_f = sb("eye_f", [128, 128], F32)
        eye_b = sb("eye_b", [128, 128], BF16)
        ones_f = sb("ones_f", [1, 128], F32)
        ones_b = sb("ones_b", [1, 128], BF16)
        wn_sb = sb("wn_sb", [128, NKD], F32)
        bdt_sb = sb("bdt_sb", [128, NKI], F32)
        b1_sb = sb("b1_sb", [128, NE * NMF], F32)
        b2_sb = sb("b2_sb", [NE, D], F32)
        state = sb("state", [128, NKI], F32)
        p = [gs.enter_context(nc.psum_tensor(f"pb{i}", [128, 512], F32))
             for i in range(8)]

        qC = mksem("qC")
        qE = mksem("qE")
        sX = [mksem(f"sX{i}") for i in range(4)]
        sW = [mksem(f"sW{i}") for i in range(2)]
        sXS = mksem("sXS")
        sXN = mksem("sXN")
        sUF = mksem("sUF")
        sSG = mksem("sSG")
        sRC = mksem("sRC")
        sRT = [mksem(f"sRT{i}") for i in range(2)]
        sIN4 = mksem("sIN4")
        sYA = mksem("sYA")
        sP5 = mksem("sP5")
        sPRE = mksem("sPRE")
        sXB = [mksem(f"sXB{i}") for i in range(2)]
        sW1 = mksem("sw1ld")
        sW2 = [mksem(f"sW2{i}") for i in range(2)]
        sMK = mksem("sMK")
        qO = [mksem(f"qO{i}") for i in range(2)]
        cT = mksem("cT")
        cV = mksem("cV")
        cS = mksem("cS")
        ENAMES = ("sync", "gpsimd", "vector", "scalar", "tensor")
        bar = {e: mksem("bar_" + e) for e in ENAMES}
        nbar = [0]

        block = gs.enter_context(nc.Block())
        B = {"sync": block.sync, "gpsimd": block.gpsimd, "vector": block.vector,
             "scalar": block.scalar, "tensor": block.tensor}

        def finish():
            barrier()
            def syfin(sy):
                for sm in (qC, qE, sXS, sXN, sUF, sSG, sRC, sIN4, sYA, sP5,
                           sPRE, sW1, sMK):
                    sy.wait_ge(sm.h, sm.n)
                for sm in sX + sW + sRT + sXB + sW2 + qO:
                    sy.wait_ge(sm.h, sm.n)
            B["sync"](syfin)

        def barrier():
            nbar[0] += 1
            k = nbar[0]
            for name in ENAMES:
                def f(eng, name=name, k=k):
                    eng.sem_inc(bar[name].h, 1)
                    for other in ENAMES:
                        if other != name:
                            eng.wait_ge(bar[other].h, k)
                B[name](f)
                bar[name].n = k

        p12 = contextlib.ExitStack()
        p12.__enter__()
        winT = sb("winT", [128, NKD * NMU * 128], BF16, p12)
        r_nmean = sb("r_nmean", [1, TOK], F32, p12)
        r_rstd = sb("r_rstd", [1, TOK], F32, p12)
        r_rstdh = sb("r_rstdh", [1, TOK], F32, p12)
        cst_rows = sb("cst_rows", [16, 128], F32, p12)
        wn_st = sb("wn_st", [8, 128], F32, p12)
        b1st = sb("b1st", [32, NE * 128], F32, p12)

        def p0_scalar(s):
            qC.inc(s.dma_start(out=eye_f[:], in_=eye_h), 16)
            qC.inc(s.dma_start(out=cst_rows[0:16, :],
                               in_=bdt_h.rearrange("(t q) -> t q", q=128)), 16)
            qC.inc(s.dma_start(out=wn_st[:],
                               in_=wn_h.rearrange("(t q) -> t q", q=128)), 16)
            for e in range(NE):
                qC.inc(s.dma_start(out=b1st[:, e * 128:(e + 1) * 128],
                                   in_=b1_h[e].rearrange("(t q) -> t q", q=128)), 16)
            qC.inc(s.dma_start(out=b2_sb[:], in_=b2_h), 16)
        B["scalar"](p0_scalar)

        def p0_g(g):
            qE.inc(g.dma_start(out=eye_b[:], in_=eye_h), 16)
        B["gpsimd"](p0_g)

        def p0_v(v):
            v.memset(ones_f[:], 1.0)
            v.memset(ones_b[:], 1.0)
            cV.inc(v.memset(state[:], 0.0))
        B["vector"](p0_v)

        p1 = contextlib.ExitStack()
        with p1:
            NWB = 2
            wst = [sb(f"wst{i}", [128, D], F32, p1) for i in range(NWB)]
            NXB = 4
            x_tl = [sb(f"x_tl{i}", [128, D], F32, p1) for i in range(NXB)]
            sq_tl = sb("sq_tl", [128, D], F32, p1)
            packed = sb("packed", [128, 4 * NT], F32, p1)
            stt = sb("stt", [128, 12], F32, p1)
            ic_t = sb("ic_t", [128, 1], I32, p1)
            r_route = sb("r_route", [1, TOK], F32, p1)
            rowt = [sb(f"rowt{i}", [1, TOK], F32, p1) for i in range(2)]
            rowt2 = sb("rowt2", [1, TOK], F32, p1)
            rows4 = sb("rows4", [4, TOK], F32, p1)

            def pk(i, c):
                return packed[:, 4 * i + c:4 * i + c + 1]

            qc_all = qC.n

            def tCst(t):
                t.wait_ge(qC.h, qc_all)
                cT.inc(t.transpose(p[0][0:128, 0:16], cst_rows[0:16, :], eye_f[0:16, 0:16]))
                cT.inc(t.transpose(p[1][0:128, 0:8], wn_st[:], eye_f[0:8, 0:8]))
                for e in range(NE):
                    bank = p[2 + e // 4]
                    off = (e % 4) * 32
                    cT.inc(t.transpose(bank[0:128, off:off + 32],
                                       b1st[:, e * 128:(e + 1) * 128],
                                       eye_f[0:32, 0:32]))
            B["tensor"](tCst)
            ct_cst = cT.n

            def vCst(v):
                v.wait_ge(cT.h, ct_cst)
                v.tensor_copy(bdt_sb[:], p[0][0:128, 0:16])
                v.tensor_copy(wn_sb[:], p[1][0:128, 0:8])
                for e in range(NE):
                    bank = p[2 + e // 4]
                    off = (e % 4) * 32
                    v.tensor_copy(b1_sb[:, e * NMF:(e + 1) * NMF],
                                  bank[0:128, off:off + 32])
                cV.inc(v.engine_nop())
            B["vector"](vCst)
            cv_cst = cV.n

            strip_done_v = [0] * NMU
            grp_evac_v = [0] * (NMU * 2)

            for mj in range(NMU):
                def gA(g, mj=mj):
                    if mj == 0:
                        g.wait_ge(qC.h, 16)
                    if mj >= NWB:
                        g.wait_ge(cV.h, strip_done_v[mj - NWB])
                    sW[mj % NWB].inc(g.dma_start(out=wst[mj % NWB][:],
                                     in_=win_h[mj * 128:(mj + 1) * 128, :]), 16)
                B["gpsimd"](gA)
                qg_strip = sW[mj % NWB].n

                for gi in range(2):
                    def tA(t, mj=mj, gi=gi, qg_strip=qg_strip):
                        if gi == 0:
                            t.wait_ge(sW[mj % NWB].h, qg_strip)
                        gidx = mj * 2 + gi
                        if gidx >= 1:
                            t.wait_ge(cV.h, grp_evac_v[gidx - 1])
                        mi = None
                        for kk in range(4):
                            ki = gi * 4 + kk
                            bank = p[4 + kk]
                            mi = t.transpose(bank[0:128, 0:128],
                                             wst[mj % NWB][:, ki * 128:(ki + 1) * 128],
                                             eye_f[:])
                        cT.inc(mi)
                    B["tensor"](tA)
                    ct_g = cT.n

                    def vA(v, mj=mj, gi=gi, ct_g=ct_g):
                        if mj == 0 and gi == 0:
                            v.wait_ge(cV.h, cv_cst)
                        v.wait_ge(cT.h, ct_g)
                        last = None
                        for kk in range(4):
                            ki = gi * 4 + kk
                            bank = p[4 + kk]
                            dst = (ki * NMU + mj) * 128
                            last = v.tensor_scalar(winT[:, dst:dst + 128],
                                                   bank[0:128, 0:128],
                                                   wn_sb[:, ki:ki + 1], None, ALU.mult)
                        n = cV.inc(last)
                        grp_evac_v[mj * 2 + gi] = n
                    B["vector"](vA)
                strip_done_v[mj] = cV.n

            relx_v = [0] * NT
            rows_evac_v = [0] * NT
            for i in range(NT):
                def syB(sy, i=i):
                    if i >= NXB:
                        sy.wait_ge(cV.h, relx_v[i - NXB])
                    sX[i % NXB].inc(sy.dma_start(out=x_tl[i % NXB][:],
                                    in_=x_h[i * 128:(i + 1) * 128, :]), 16)
                B["sync"](syB)
                qs_here = sX[i % NXB].n

                def vB1(v, i=i, qs_here=qs_here):
                    xt = x_tl[i % NXB]
                    v.wait_ge(sX[i % NXB].h, qs_here)
                    v.reduce_sum(stt[:, 0:1], xt[:], AX.X)
                    v.tensor_tensor(sq_tl[:], xt[:], xt[:], ALU.mult)
                    v.reduce_sum(stt[:, 1:2], sq_tl[:], AX.X)
                    v.tensor_single_scalar(pk(i, 0), stt[:, 0:1], -1.0 / D, ALU.mult)
                    v.tensor_single_scalar(stt[:, 2:3], stt[:, 0:1], 1.0 / D, ALU.mult)
                    v.drain()
                    v.tensor_tensor(stt[:, 3:4], stt[:, 2:3], stt[:, 2:3], ALU.mult)
                    v.tensor_single_scalar(stt[:, 4:5], stt[:, 1:2], 1.0 / D, ALU.mult)
                    v.drain()
                    v.tensor_tensor(stt[:, 4:5], stt[:, 4:5], stt[:, 3:4], ALU.subtract)
                    v.drain()
                    v.tensor_single_scalar(stt[:, 4:5], stt[:, 4:5], EPS, ALU.add)
                    v.drain()
                    v.tensor_single_scalar(stt[:, 5:6], stt[:, 4:5], 1.0 + EPS,
                                           ALU.mult)
                    v.drain()
                    v.tensor_single_scalar(stt[:, 5:6], stt[:, 5:6], -EPS, ALU.add)
                    v.drain()
                    n = cV.inc(v.reduce_sum(stt[:, 6:7], xt[:, 0:2], AX.X))
                    relx_v[i] = n
                B["vector"](vB1)
                cv1 = cV.n

                def sB1(s, i=i, cv1=cv1):
                    s.wait_ge(cV.h, cv1)
                    s.activation(stt[:, 7:9], stt[:, 4:6], AF.Ln)
                    s.drain()
                    cS.inc(s.activation(packed[:, 4 * i + 1:4 * i + 3],
                                        stt[:, 7:9], AF.Exp, scale=-0.5))
                B["scalar"](sB1)
                cs1 = cS.n

                def vB2(v, i=i, cs1=cs1):
                    v.wait_ge(cS.h, cs1)
                    v.tensor_single_scalar(stt[:, 9:10], pk(i, 0), 2.0, ALU.mult)
                    v.drain()
                    v.tensor_tensor(stt[:, 9:10], stt[:, 6:7], stt[:, 9:10], ALU.add)
                    v.drain()
                    cV.inc(v.tensor_tensor(stt[:, 9:10], stt[:, 9:10], pk(i, 1),
                                           ALU.mult))
                B["vector"](vB2)
                cv2 = cV.n

                def sB2(s, cv2=cv2):
                    s.wait_ge(cV.h, cv2)
                    s.activation(ic_t[:], stt[:, 9:10], AF.Copy)
                    s.drain()
                    cS.inc(s.activation(stt[:, 10:11], ic_t[:], AF.Copy))
                B["scalar"](sB2)
                cs2 = cS.n

                def vB3(v, cs2=cs2):
                    v.wait_ge(cS.h, cs2)
                    f = stt[:, 10:11]
                    vr = stt[:, 9:10]
                    v.tensor_tensor(stt[:, 11:12], f, vr, ALU.is_gt)
                    v.drain()
                    v.tensor_tensor(f, f, stt[:, 11:12], ALU.subtract)
                    v.tensor_single_scalar(stt[:, 11:12], vr, 0.0, ALU.is_lt)
                    v.drain()
                    v.tensor_tensor(stt[:, 2:3], vr, f, ALU.not_equal)
                    v.drain()
                    v.tensor_tensor(stt[:, 11:12], stt[:, 11:12], stt[:, 2:3], ALU.mult)
                    v.drain()
                    v.tensor_tensor(f, f, stt[:, 11:12], ALU.add)
                    v.drain()
                    cV.inc(v.tensor_single_scalar(stt[:, 2:3], f, 0.125, ALU.mult))
                B["vector"](vB3)
                cv3 = cV.n

                def sB3(s, cv3=cv3):
                    s.wait_ge(cV.h, cv3)
                    s.activation(ic_t[:], stt[:, 2:3], AF.Copy)
                    s.drain()
                    cS.inc(s.activation(stt[:, 3:4], ic_t[:], AF.Copy))
                B["scalar"](sB3)
                cs3 = cS.n

                def vB4(v, i=i, cs3=cs3):
                    v.wait_ge(cS.h, cs3)
                    f2 = stt[:, 3:4]
                    u8 = stt[:, 2:3]
                    tr = stt[:, 10:11]
                    v.tensor_tensor(stt[:, 11:12], f2, u8, ALU.is_gt)
                    v.drain()
                    v.tensor_tensor(f2, f2, stt[:, 11:12], ALU.subtract)
                    v.drain()
                    v.tensor_single_scalar(f2, f2, -8.0, ALU.mult)
                    v.drain()
                    cV.inc(v.tensor_tensor(pk(i, 3), tr, f2, ALU.add))
                B["vector"](vB4)
                cv4 = cV.n

                def tB(t, i=i, cv4=cv4):
                    t.wait_ge(cV.h, cv4)
                    if i == 0:
                        t.wait_ge(qC.h, qc_all)
                    if i > 0:
                        t.wait_ge(cV.h, rows_evac_v[i - 1])
                    cT.inc(t.transpose(p[0][0:4, 0:128],
                                       packed[:, 4 * i:4 * i + 4], eye_f[:]))
                B["tensor"](tB)
                ct_b = cT.n

                def vB5(v, i=i, ct_b=ct_b):
                    v.wait_ge(cT.h, ct_b)
                    n = cV.inc(v.tensor_copy(rows4[0:4, i * 128:(i + 1) * 128],
                                             p[0][0:4, 0:128]))
                    rows_evac_v[i] = n
                B["vector"](vB5)

            def syC(sy):
                sy.wait_ge(cV.h, rows_evac_v[NT - 1])
                sRC.inc(sy.dma_start(out=r_nmean[:], in_=rows4[0:1, :]), 16)
                sRC.inc(sy.dma_start(out=r_rstd[:], in_=rows4[1:2, :]), 16)
                sRC.inc(sy.dma_start(out=r_rstdh[:], in_=rows4[2:3, :]), 16)
                sRC.inc(sy.dma_start(out=r_route[:], in_=rows4[3:4, :]), 16)
            B["sync"](syC)
            qs_rows = sRC.n

            items = [("rowa", None), ("rowb", None)] + [("mask", e) for e in range(NE)]
            tmp_free_qs = [0, 0]
            for k, (kind, e) in enumerate(items):
                def vCk(v, k=k, kind=kind, e=e):
                    if k < 2:
                        v.wait_ge(sRC.h, qs_rows)
                    if tmp_free_qs[k % 2]:
                        v.wait_ge(sRT[k % 2].h, tmp_free_qs[k % 2])
                    dst = rowt[k % 2]
                    if kind == "rowa":
                        cV.inc(v.tensor_single_scalar(dst[:], r_rstd[:], 1.0, ALU.add))
                    elif kind == "rowb":
                        cV.inc(v.tensor_tensor(dst[:], r_nmean[:], r_rstd[:], ALU.mult))
                    else:
                        v.tensor_single_scalar(dst[:], r_route[:], float(e),
                                               ALU.is_equal)
                        v.tensor_single_scalar(rowt2[:], r_route[:],
                                               float((e - 1) % NE), ALU.is_equal)
                        v.tensor_tensor(dst[:], dst[:], rowt2[:], ALU.add)
                        cV.inc(v.tensor_single_scalar(dst[:], dst[:], 0.5, ALU.mult))
                B["vector"](vCk)
                cv_k = cV.n

                def syCk(sy, k=k, kind=kind, e=e, cv_k=cv_k):
                    sy.wait_ge(cV.h, cv_k)
                    dst = (rows_s[0:1, :] if kind == "rowa" else
                           rows_s[1:2, :] if kind == "rowb" else
                           mask_s[e:e + 1, :])
                    sRT[k % 2].inc(sy.dma_start(out=dst, in_=rowt[k % 2][:]), 16)
                B["sync"](syCk)
                tmp_free_qs[k % 2] = sRT[k % 2].n

            if debug:
                def syDbg(sy):
                    sy.wait_ge(sRT[0].h, sRT[0].n)
                    sy.wait_ge(sRT[1].h, sRT[1].n)
                    sRC.inc(sy.dma_start(out=dbg_pk, in_=packed[:]), 16)
                    sRC.inc(sy.dma_start(out=dbg_rows[0:1, :], in_=r_nmean[:]), 16)
                    sRC.inc(sy.dma_start(out=dbg_rows[1:2, :], in_=r_rstd[:]), 16)
                    sRC.inc(sy.dma_start(out=dbg_rows[2:3, :], in_=r_rstdh[:]), 16)
                    sRC.inc(sy.dma_start(out=dbg_rows[3:4, :], in_=r_route[:]), 16)
                B["sync"](syDbg)
        qs_masks_rt = (sRT[0].n, sRT[1].n)

        barrier()
        if upto <= 1:
            finish()
            p12.__exit__(None, None, None)
            return nc

        p2 = contextlib.ExitStack()
        with p2:
            x_st = sb("x_st", [128, 4 * D], F32, p2)
            xT = sb("xT", [128, NKD * ST], F32, p2)
            bA = [sb(f"bA{i}", [128, ST], F32, p2) for i in range(3)]
            tb = sb("tb", [128, ST], F32, p2)
            xnb_st = sb("xnb_st", [128, NKD * ST], BF16, p2)
            hb_st = sb("hb_st", [128, NKD * ST], BF16, p2)
            ubf_st = sb("ubf_st", [128, NKI * ST], BF16, p2)
            ufh_st = sb("ufh_st", [128, 8 * ST], F32, p2)
            szb_st = sb("szb_st", [128, NKI * ST], BF16, p2)

            x_rel_t = 0
            norm_done_qs = 0
            stage_done_qs = 0
            ufh_done_qs = 0

            for s in range(NST):
                def sy0(sy, s=s, xrel=x_rel_t):
                    if s > 0:
                        sy.wait_ge(cT.h, xrel)
                    sXS.inc(sy.dma_start(
                        out=x_st[:].rearrange("q (j d) -> q j d", j=4),
                        in_=x_h[s * ST:(s + 1) * ST, :].rearrange(
                            "(j q) d -> q j d", q=128)), 16)
                B["sync"](sy0)
                qs_x = sXS.n

                xg_v = [0] * 8
                for g in range(8):
                    j, half = g // 2, g % 2
                    def tX(t, j=j, half=half, g=g, qs_x=qs_x):
                        if g == 0:
                            t.wait_ge(sXS.h, qs_x)
                        if g >= 1:
                            t.wait_ge(cV.h, xg_v[g - 1])
                        mi = None
                        for kk in range(4):
                            kd = half * 4 + kk
                            mi = t.transpose(p[kk][0:128, 0:128],
                                             x_st[:, j * D + kd * 128: j * D + (kd + 1) * 128],
                                             eye_f[:])
                        cT.inc(mi)
                    B["tensor"](tX)
                    ct_g = cT.n

                    def vX(v, j=j, half=half, g=g, ct_g=ct_g):
                        v.wait_ge(cT.h, ct_g)
                        last = None
                        for kk in range(4):
                            kd = half * 4 + kk
                            last = v.tensor_copy(
                                xT[:, kd * ST + j * 128: kd * ST + (j + 1) * 128],
                                p[kk][0:128, 0:128])
                        n = cV.inc(last)
                        xg_v[g] = n
                    B["vector"](vX)
                x_rel_t = cT.n

                def tBc(t, s=s, last_xe=cV.n):
                    t.wait_ge(cV.h, last_xe)
                    for r_i, row in enumerate((r_nmean, r_rstd, r_rstdh)):
                        cT.inc(t.matmul(p[4 + r_i][:], ones_f[:],
                                        row[:, s * ST:(s + 1) * ST],
                                        start=True, stop=True))
                B["tensor"](tBc)
                ct_bc = cT.n

                def vN(v, ct_bc=ct_bc, ndq=norm_done_qs):
                    v.wait_ge(cT.h, ct_bc)
                    if ndq:
                        v.wait_ge(sXN.h, ndq)
                    for r_i in range(3):
                        v.tensor_copy(bA[r_i][:], p[4 + r_i][:])
                    last = None
                    for kd in range(NKD):
                        sl = slice(kd * ST, (kd + 1) * ST)
                        v.tensor_tensor(tb[:], xT[:, sl], bA[0][:], ALU.add)
                        v.tensor_tensor(xnb_st[:, sl], tb[:], bA[1][:], ALU.mult)
                        last = v.tensor_tensor(hb_st[:, sl], tb[:], bA[2][:], ALU.mult)
                    cV.inc(last)
                B["vector"](vN)
                cv_norm = cV.n

                def syN(sy, s=s, cv_norm=cv_norm):
                    sy.wait_ge(cV.h, cv_norm)
                    sXN.inc(sy.dma_start(out=xnb_s[s], in_=xnb_st[:]), 16)
                B["sync"](syN)
                norm_done_qs = sXN.n

                ev_s = [0] * NMU
                for mj in range(NMU):
                    def tU(t, mj=mj, cv_norm=cv_norm):
                        if mj == 0:
                            t.wait_ge(cV.h, cv_norm)
                        if mj >= 2:
                            t.wait_ge(cS.h, ev_s[mj - 2])
                        mi = None
                        for ki in range(NKD):
                            mi = t.matmul(p[mj % 2][:],
                                          winT[:, (ki * NMU + mj) * 128:(ki * NMU + mj + 1) * 128],
                                          hb_st[:, ki * ST:(ki + 1) * ST],
                                          start=(ki == 0), stop=(ki == NKD - 1))
                        cT.inc(mi)
                    B["tensor"](tU)
                    ct_u = cT.n

                    def sU(s_, mj=mj, ct_u=ct_u, sdq=stage_done_qs,
                           hdq=ufh_done_qs):
                        s_.wait_ge(cT.h, ct_u)
                        if mj == 0 and sdq:
                            s_.wait_ge(sSG.h, sdq)
                        if mj == 0 and hdq:
                            s_.wait_ge(sUF.h, hdq)
                        if mj == 8 and hdq:
                            s_.wait_ge(sUF.h, hdq)
                        bank = p[mj % 2]
                        if mj < NMJ:
                            sl = slice(mj * ST, (mj + 1) * ST)
                            hsl = slice((mj % 8) * ST, (mj % 8 + 1) * ST)
                            s_.activation(ubf_st[:, sl], bank[:], AF.Silu)
                            n = cS.inc(s_.activation(ufh_st[:, hsl], bank[:], AF.Silu))
                        else:
                            sl = slice((mj - NMJ) * ST, (mj - NMJ + 1) * ST)
                            n = cS.inc(s_.activation(szb_st[:, sl], bank[:], AF.Silu))
                        ev_s[mj] = n
                    B["scalar"](sU)

                    if mj == 7:
                        def syH1(sy, s=s, tgt=cS.n):
                            sy.wait_ge(cS.h, tgt)
                            sUF.inc(sy.dma_start(out=uf_s[s, :, 0:8 * ST],
                                                 in_=ufh_st[:]), 16)
                        B["sync"](syH1)
                        ufh_done_qs = sUF.n
                    elif mj == 15:
                        def syH2(sy, s=s, tgt=cS.n):
                            sy.wait_ge(cS.h, tgt)
                            sUF.inc(sy.dma_start(out=uf_s[s, :, 8 * ST:16 * ST],
                                                 in_=ufh_st[:]), 16)
                        B["sync"](syH2)
                        ufh_done_qs = sUF.n

                def syU(sy, s=s):
                    sy.wait_ge(cS.h, ev_s[NMU - 1])
                    sSG.inc(sy.dma_start(out=ubf_s[s], in_=ubf_st[:]), 16)
                    sSG.inc(sy.dma_start(out=szb_s[s], in_=szb_st[:]), 16)
                B["sync"](syU)
                stage_done_qs = sSG.n

        barrier()
        p12.__exit__(None, None, None)
        if upto <= 2:
            finish()
            return nc

        p34 = contextlib.ExitStack()
        p34.__enter__()
        wdtT = sb("wdtT", [128, NKI * NMJ * 128], BF16, p34)
        woutT = sb("woutT", [128, NKI * NMD * 128], BF16, p34)
        p3 = contextlib.ExitStack()
        with p3:
            NWB3 = 2
            wst3 = [sb(f"wst3_{i}", [128, INNER], F32, p3) for i in range(NWB3)]
            jobs = ([("dt", mj) for mj in range(NMJ)] +
                    [("out", mj) for mj in range(NMD)])
            strip_rel3 = [0] * len(jobs)
            grp_ev3 = [0] * (len(jobs) * 4)
            for jn, (kind, mj) in enumerate(jobs):
                def gP3(g, jn=jn, kind=kind, mj=mj):
                    if jn >= NWB3:
                        g.wait_ge(cV.h, strip_rel3[jn - NWB3])
                    src = (wdt_h[mj * 128:(mj + 1) * 128, :] if kind == "dt"
                           else wout_h[mj * 128:(mj + 1) * 128, :])
                    sW[jn % NWB3].inc(g.dma_start(out=wst3[jn % NWB3][:], in_=src), 16)
                B["gpsimd"](gP3)
                qg_j = sW[jn % NWB3].n

                for gi in range(4):
                    def tP3(t, jn=jn, gi=gi, qg_j=qg_j):
                        if gi == 0:
                            t.wait_ge(sW[jn % NWB3].h, qg_j)
                        gidx = jn * 4 + gi
                        if gidx >= 2:
                            t.wait_ge(cV.h, grp_ev3[gidx - 2])
                        mi = None
                        for kk in range(4):
                            ki = gi * 4 + kk
                            bank = p[(gidx % 2) * 4 + kk]
                            mi = t.transpose(bank[0:128, 0:128],
                                             wst3[jn % NWB3][:, ki * 128:(ki + 1) * 128],
                                             eye_f[:])
                        cT.inc(mi)
                    B["tensor"](tP3)
                    ct_g = cT.n

                    def vP3(v, jn=jn, gi=gi, kind=kind, mj=mj, ct_g=ct_g):
                        v.wait_ge(cT.h, ct_g)
                        gidx = jn * 4 + gi
                        last = None
                        for kk in range(4):
                            ki = gi * 4 + kk
                            bank = p[(gidx % 2) * 4 + kk]
                            if kind == "dt":
                                dst = (ki * NMJ + mj) * 128
                                last = v.tensor_copy(wdtT[:, dst:dst + 128],
                                                     bank[0:128, 0:128])
                            else:
                                dst = (ki * NMD + mj) * 128
                                last = v.tensor_copy(woutT[:, dst:dst + 128],
                                                     bank[0:128, 0:128])
                        n = cV.inc(last)
                        grp_ev3[gidx] = n
                    B["vector"](vP3)
                strip_rel3[jn] = cV.n

        barrier()
        if upto <= 3:
            finish()
            p34.__exit__(None, None, None)
            return nc

        p4 = contextlib.ExitStack()
        with p4:
            ubf4 = sb("ubf4", [128, NKI * ST], BF16, p4)
            uf4 = sb("uf4", [128, NKI * ST], F32, p4)
            szb4 = sb("szb4", [128, NKI * ST], BF16, p4)
            pbf4 = sb("pbf4", [128, NKI * ST], BF16, p4)
            lam4 = [sb(f"lam4_{i}", [128, ST], F32, p4) for i in range(2)]
            bt4 = sb("bt4", [128, ST], F32, p4)
            st4 = sb("st4", [128, ST], F32, p4)
            ya4 = sb("ya4", [128, NKD * ST], F32, p4)

            ya_done_qs = 0
            prod_v_prev = 0
            qs_sg_done = sSG.n
            qs_uf_done = sUF.n
            for s in range(NST):
                def sy4(sy, s=s, pvp=prod_v_prev):
                    if s == 0:
                        sy.wait_ge(sSG.h, qs_sg_done)
                        sy.wait_ge(sUF.h, qs_uf_done)
                    if pvp:
                        sy.wait_ge(cV.h, pvp)
                    sIN4.inc(sy.dma_start(out=ubf4[:], in_=ubf_s[s]), 16)
                    sIN4.inc(sy.dma_start(out=uf4[:], in_=uf_s[s]), 16)
                    sIN4.inc(sy.dma_start(out=szb4[:], in_=szb_s[s]), 16)
                B["sync"](sy4)
                in_done_qs = sIN4.n

                prod_v = [0] * NMJ
                lam_rel_v = [0] * NMJ
                sig_s = [0] * NMJ
                for mj in range(NMJ):
                    def t4(t, mj=mj, idq=in_done_qs):
                        if mj == 0:
                            t.wait_ge(sIN4.h, idq)
                        if mj >= 2:
                            t.wait_ge(cS.h, sig_s[mj - 2])
                        mi = None
                        for ki in range(NKI):
                            mi = t.matmul(p[mj % 2][:],
                                          wdtT[:, (ki * NMJ + mj) * 128:(ki * NMJ + mj + 1) * 128],
                                          ubf4[:, ki * ST:(ki + 1) * ST],
                                          start=(ki == 0), stop=(ki == NKI - 1))
                        cT.inc(mi)
                    B["tensor"](t4)
                    ct_l = cT.n

                    def s4(s_, mj=mj, ct_l=ct_l):
                        s_.wait_ge(cT.h, ct_l)
                        if mj >= 2:
                            s_.wait_ge(cV.h, lam_rel_v[mj - 2])
                        n = cS.inc(s_.activation(lam4[mj % 2][:], p[mj % 2][:],
                                                 AF.Sigmoid,
                                                 bias=bdt_sb[:, mj:mj + 1]))
                        sig_s[mj] = n
                    B["scalar"](s4)
                    cs_sig = cS.n

                    def v4(v, mj=mj, cs_sig=cs_sig):
                        v.wait_ge(cS.h, cs_sig)
                        lam = lam4[mj % 2]
                        sl = slice(mj * ST, (mj + 1) * ST)
                        v.tensor_single_scalar(bt4[:], lam[:], -1.0, ALU.mult)
                        v.drain()
                        v.tensor_single_scalar(bt4[:], bt4[:], 1.0, ALU.add)
                        v.drain()
                        v.tensor_tensor(bt4[:], bt4[:], uf4[:, sl], ALU.mult)
                        v.drain()
                        v.tensor_tensor_scan(st4[:], lam[:], bt4[:],
                                             state[:, mj:mj + 1], ALU.mult, ALU.add)
                        n = cV.inc(v.tensor_tensor(pbf4[:, sl], st4[:], szb4[:, sl],
                                                   ALU.mult))
                        prod_v[mj] = n
                        n = cV.inc(v.tensor_copy(state[:, mj:mj + 1],
                                                 st4[:, ST - 1:ST]))
                        lam_rel_v[mj] = n
                    B["vector"](v4)

                yev_v = [0] * NMD
                for mj in range(NMD):
                    def t4o(t, mj=mj):
                        if mj == 0:
                            t.wait_ge(cV.h, prod_v[NMJ - 1])
                        if mj >= 2:
                            t.wait_ge(cV.h, yev_v[mj - 2])
                        mi = None
                        for ki in range(NKI):
                            mi = t.matmul(p[2 + mj % 2][:],
                                          woutT[:, (ki * NMD + mj) * 128:(ki * NMD + mj + 1) * 128],
                                          pbf4[:, ki * ST:(ki + 1) * ST],
                                          start=(ki == 0), stop=(ki == NKI - 1))
                        cT.inc(mi)
                    B["tensor"](t4o)
                    ct_y = cT.n

                    def v4o(v, mj=mj, ct_y=ct_y, ydq=ya_done_qs):
                        v.wait_ge(cT.h, ct_y)
                        if mj == 0 and ydq:
                            v.wait_ge(sYA.h, ydq)
                        n = cV.inc(v.tensor_copy(ya4[:, mj * ST:(mj + 1) * ST],
                                                 p[2 + mj % 2][:]))
                        yev_v[mj] = n
                    B["vector"](v4o)

                def sy4o(sy, s=s):
                    sy.wait_ge(cV.h, yev_v[NMD - 1])
                    sYA.inc(sy.dma_start(out=ya_s[s], in_=ya4[:]), 16)
                B["sync"](sy4o)
                ya_done_qs = sYA.n
                prod_v_prev = prod_v[NMJ - 1]

        barrier()
        p34.__exit__(None, None, None)
        if upto <= 4:
            finish()
            return nc

        acc = sb("acc", [128, NST * NMD * ST], F32)
        qs_ya_done = sYA.n
        p5 = contextlib.ExitStack()
        with p5:
            x5 = sb("x5", [128, 4 * D], F32, p5)
            xT5 = sb("xT5", [128, NKD * ST], F32, p5)
            ya5 = sb("ya5", [128, NKD * ST], F32, p5)
            rab = sb("rab", [128, ST], F32, p5)
            rbb = sb("rbb", [128, ST], F32, p5)
            rowa5 = sb("rowa5", [1, TOK], F32, p5)
            rowb5 = sb("rowb5", [1, TOK], F32, p5)
            masks5 = sb("masks5", [NE, TOK], F32, p5)

            def sy5pre(sy):
                sy.wait_ge(sRT[0].h, qs_masks_rt[0])
                sy.wait_ge(sRT[1].h, qs_masks_rt[1])
                sPRE.inc(sy.dma_start(out=rowa5[:], in_=rows_s[0:1, :]), 16)
                sPRE.inc(sy.dma_start(out=rowb5[:], in_=rows_s[1:2, :]), 16)
                sPRE.inc(sy.dma_start(out=masks5[:], in_=mask_s), 16)
            B["sync"](sy5pre)
            qs5pre = sPRE.n

            acc_v = 0
            for c in range(NST):
                def sy5(sy, c=c, av=acc_v):
                    if c == 0:
                        sy.wait_ge(sYA.h, qs_ya_done)
                    if av:
                        sy.wait_ge(cV.h, av)
                    sP5.inc(sy.dma_start(out=x5[:].rearrange("q (j d) -> q j d", j=4),
                                 in_=x_h[c * ST:(c + 1) * ST, :].rearrange(
                                     "(j q) d -> q j d", q=128)), 16)
                    sP5.inc(sy.dma_start(out=ya5[:], in_=ya_s[c]), 16)
                B["sync"](sy5)
                qs5 = sP5.n

                xg5_v = [0] * 8
                for g in range(8):
                    j, half = g // 2, g % 2
                    def tX5(t, j=j, half=half, g=g, qs5=qs5):
                        if g == 0:
                            t.wait_ge(sP5.h, qs5)
                        if g >= 1:
                            t.wait_ge(cV.h, xg5_v[g - 1])
                        mi = None
                        for kk in range(4):
                            kd = half * 4 + kk
                            mi = t.transpose(p[kk][0:128, 0:128],
                                             x5[:, j * D + kd * 128: j * D + (kd + 1) * 128],
                                             eye_f[:])
                        cT.inc(mi)
                    B["tensor"](tX5)
                    ct_g5 = cT.n

                    def vX5(v, j=j, half=half, g=g, ct_g5=ct_g5):
                        v.wait_ge(cT.h, ct_g5)
                        last = None
                        for kk in range(4):
                            kd = half * 4 + kk
                            last = v.tensor_copy(
                                xT5[:, kd * ST + j * 128: kd * ST + (j + 1) * 128],
                                p[kk][0:128, 0:128])
                        n = cV.inc(last)
                        xg5_v[g] = n
                    B["vector"](vX5)

                def t5b(t, c=c, last_xe=cV.n, qs5pre=qs5pre):
                    t.wait_ge(cV.h, last_xe)
                    if c == 0:
                        t.wait_ge(sPRE.h, qs5pre)
                    cT.inc(t.matmul(p[4][:], ones_f[:],
                                    rowa5[:, c * ST:(c + 1) * ST],
                                    start=True, stop=True))
                    cT.inc(t.matmul(p[5][:], ones_f[:],
                                    rowb5[:, c * ST:(c + 1) * ST],
                                    start=True, stop=True))
                B["tensor"](t5b)
                ct5b = cT.n

                def v5b(v, ct5b=ct5b):
                    v.wait_ge(cT.h, ct5b)
                    v.tensor_copy(rab[:], p[4][:])
                    cV.inc(v.tensor_copy(rbb[:], p[5][:]))
                B["vector"](v5b)
                cv5b = cV.n

                def t5m(t, c=c, cv5b=cv5b):
                    t.wait_ge(cV.h, cv5b)
                    for m in range(NMD):
                        cT.inc(t.matmul(p[m][:], b2_sb[:, m * 128:(m + 1) * 128],
                                        masks5[:, c * ST:(c + 1) * ST],
                                        start=True, stop=True))
                B["tensor"](t5m)
                ct5m = cT.n

                def v5m(v, c=c, ct5m=ct5m):
                    v.wait_ge(cT.h, ct5m)
                    for m in range(NMD):
                        col = (c * NMD + m) * ST
                        a = acc[:, col:col + ST]
                        v.tensor_tensor(a, xT5[:, m * ST:(m + 1) * ST], rab[:],
                                        ALU.mult)
                        v.tensor_tensor(a, a, rbb[:], ALU.add)
                        v.tensor_tensor(a, a, p[m][:], ALU.add)
                        cV.inc(v.tensor_tensor(a, a, ya5[:, m * ST:(m + 1) * ST],
                                               ALU.add))
                B["vector"](v5m)
                acc_v = cV.n

        barrier()
        if upto <= 5:
            finish()
            return nc

        p6 = contextlib.ExitStack()
        with p6:
            w1sb = sb("w1sb", [128, NKD * DFF], BF16, p6)
            h1 = sb("h1", [128, NMF * ST], BF16, p6)
            w2sb = [sb(f"w2sb{i}", [128, 4 * D], BF16, p6) for i in range(2)]
            xnb6 = [sb(f"xnb6_{i}", [128, NKD * ST], BF16, p6) for i in range(2)]
            mkrow = sb("mkrow", [1, TOK], BF16, p6)
            mkb = [sb(f"mkb{i}", [128, ST], BF16, p6) for i in range(2)]
            tmp6 = sb("tmp6", [128, ST], F32, p6)

            w1_rel_t = 0
            xnb_rel_t = [0, 0]
            mkrow_rel_t = [0]
            w2_rel_t = {}
            evac_all_v = 0

            for e in range(NE):
                def g6w1(g, e=e, rel=w1_rel_t):
                    if rel:
                        g.wait_ge(cT.h, rel)
                    for ki in range(NKD):
                        sW1.inc(g.dma_start(out=w1sb[:, ki * DFF:(ki + 1) * DFF],
                                            in_=w1_h[e, ki * 128:(ki + 1) * 128, :]), 16)
                B["gpsimd"](g6w1)
                qg_w1 = sW1.n

                def sy6m(g, e=e, rel=mkrow_rel_t[0]):
                    if rel:
                        g.wait_ge(cT.h, rel)
                    if e == 0:
                        g.wait_ge(sRT[0].h, sRT[0].n)
                        g.wait_ge(sRT[1].h, sRT[1].n)
                    sMK.inc(g.dma_start(out=mkrow[:], in_=mask_s[e:e + 1, :]), 16)
                B["gpsimd"](sy6m)
                qs_mrow = sMK.n

                for c in range(NST):
                    u = e * NST + c

                    def sy6(sy, c=c, u=u, rel=xnb_rel_t[u % 2]):
                        if rel:
                            sy.wait_ge(cT.h, rel)
                        if u < 2:
                            sy.wait_ge(sXN.h, sXN.n)
                        sXB[u % 2].inc(sy.dma_start(out=xnb6[u % 2][:],
                                                    in_=xnb_s[c]), 16)
                    B["sync"](sy6)
                    qs_xnb = sXB[u % 2].n

                    def t6m(t, e=e, c=c, eav=evac_all_v, qs_mrow=qs_mrow):
                        if eav:
                            t.wait_ge(cV.h, eav)
                        if c == 0:
                            t.wait_ge(sMK.h, qs_mrow)
                        cT.inc(t.matmul(p[2][:], ones_b[:],
                                        mkrow[:, c * ST:(c + 1) * ST],
                                        start=True, stop=True))
                    B["tensor"](t6m)
                    ct6m = cT.n
                    if c == NST - 1:
                        mkrow_rel_t[0] = cT.n

                    def v6m(v, u=u, ct6m=ct6m):
                        v.wait_ge(cT.h, ct6m)
                        cV.inc(v.tensor_copy(mkb[u % 2][:], p[2][:]))
                    B["vector"](v6m)

                    h1_s = [0] * NMF
                    for mj in range(NMF):
                        def t6p1(t, u=u, mj=mj, qg_w1=qg_w1, qs_xnb=qs_xnb):
                            if mj == 0:
                                t.wait_ge(sW1.h, qg_w1)
                                t.wait_ge(sXB[u % 2].h, qs_xnb)
                            if mj >= 2:
                                t.wait_ge(cS.h, h1_s[mj - 2])
                            mi = None
                            for ki in range(NKD):
                                mi = t.matmul(p[mj % 2][:],
                                              w1sb[:, ki * DFF + mj * 128: ki * DFF + (mj + 1) * 128],
                                              xnb6[u % 2][:, ki * ST:(ki + 1) * ST],
                                              start=(ki == 0), stop=(ki == NKD - 1))
                            cT.inc(mi)
                        B["tensor"](t6p1)
                        ct_p1 = cT.n

                        def s6(s_, e=e, mj=mj, ct_p1=ct_p1):
                            s_.wait_ge(cT.h, ct_p1)
                            col = e * NMF + mj
                            n = cS.inc(s_.activation(h1[:, mj * ST:(mj + 1) * ST],
                                                     p[mj % 2][:], AF.Relu,
                                                     bias=b1_sb[:, col:col + 1]))
                            h1_s[mj] = n
                        B["scalar"](s6)
                    if c == NST - 1:
                        w1_rel_t = cT.n
                    xnb_rel_t[u % 2] = cT.n

                    for kg in range(8):
                        def g6w2(g, e=e, u=u, kg=kg):
                            key = (u - 1, 6 + kg) if kg < 2 else (u, kg - 2)
                            if key in w2_rel_t:
                                g.wait_ge(cT.h, w2_rel_t[key])
                            sW2[kg % 2].inc(g.dma_start(
                                out=w2sb[kg % 2][:].rearrange("q (kc d) -> q kc d", kc=4),
                                in_=w2_h[e, kg * 512:(kg + 1) * 512, :].rearrange(
                                    "(kc q) d -> q kc d", q=128)), 16)
                        B["gpsimd"](g6w2)
                        qg_w2 = sW2[kg % 2].n

                        def t6p2(t, kg=kg, qg_w2=qg_w2, cs_h1=cS.n, cv_now=cV.n):
                            t.wait_ge(sW2[kg % 2].h, qg_w2)
                            if kg == 0:
                                t.wait_ge(cS.h, cs_h1)
                                t.wait_ge(cV.h, cv_now)
                            mi = None
                            for kc in range(4):
                                kt = kg * 4 + kc
                                for m in range(NMD):
                                    mi = t.matmul(p[m][:],
                                                  w2sb[kg % 2][:, kc * D + m * 128: kc * D + (m + 1) * 128],
                                                  h1[:, kt * ST:(kt + 1) * ST],
                                                  start=(kt == 0), stop=(kt == 31))
                            cT.inc(mi)
                        B["tensor"](t6p2)
                        w2_rel_t[(u, kg)] = cT.n
                    ct_p2 = cT.n

                    def v6e(v, c=c, u=u, ct_p2=ct_p2):
                        v.wait_ge(cT.h, ct_p2)
                        last = None
                        for m in range(NMD):
                            col = (c * NMD + m) * ST
                            v.tensor_tensor(tmp6[:], p[m][:], mkb[u % 2][:], ALU.mult)
                            last = v.tensor_tensor(acc[:, col:col + ST],
                                                   acc[:, col:col + ST],
                                                   tmp6[:], ALU.add)
                        cV.inc(last)
                    B["vector"](v6e)
                    evac_all_v = cV.n

        barrier()
        if upto <= 6:
            finish()
            return nc

        def syOut(sy):
            for ti in range(NT):
                qO[ti % 2].inc(sy.dma_start(
                    out=out_h[:, ti * 1024:(ti + 1) * 1024],
                    in_=acc[:, ti * 1024:(ti + 1) * 1024]), 16)
            sy.wait_ge(qO[0].h, qO[0].n)
            sy.wait_ge(qO[1].h, qO[1].n)
            for sm in (qC, qE, sXS, sXN, sUF, sSG, sRC, sIN4, sYA, sP5,
                       sPRE, sW1, sMK):
                sy.wait_ge(sm.h, sm.n)
            for sm in sX + sW + sRT + sXB + sW2:
                sy.wait_ge(sm.h, sm.n)
        B["sync"](syOut)
        barrier()

        return nc

        p2 = contextlib.ExitStack()
        with p2:
            x_st = sb("x_st", [128, 4 * D], F32, p2)
            xT = sb("xT", [128, NKD * ST], F32, p2)
            bA = [sb(f"bA{i}", [128, ST], F32, p2) for i in range(3)]
            tb = sb("tb", [128, ST], F32, p2)
            xnb_st = sb("xnb_st", [128, NKD * ST], BF16, p2)
            hb_st = sb("hb_st", [128, NKD * ST], BF16, p2)
            ubf_st = sb("ubf_st", [128, NKI * ST], BF16, p2)
            ufh_st = sb("ufh_st", [128, 8 * ST], F32, p2)
            szb_st = sb("szb_st", [128, NKI * ST], BF16, p2)

            x_rel_t = 0
            norm_done_qs = 0
            stage_done_qs = 0
            ufh_done_qs = 0

            for s in range(NST):
                def sy0(sy, s=s, xrel=x_rel_t):
                    if s > 0:
                        sy.wait_ge(cT.h, xrel)
                    sXS.inc(sy.dma_start(
                        out=x_st[:].rearrange("q (j d) -> q j d", j=4),
                        in_=x_h[s * ST:(s + 1) * ST, :].rearrange(
                            "(j q) d -> q j d", q=128)), 16)
                B["sync"](sy0)
                qs_x = sXS.n

                xg_v = [0] * 8
                for g in range(8):
                    j, half = g // 2, g % 2
                    def tX(t, j=j, half=half, g=g, qs_x=qs_x):
                        if g == 0:
                            t.wait_ge(sXS.h, qs_x)
                        if g >= 1:
                            t.wait_ge(cV.h, xg_v[g - 1])
                        mi = None
                        for kk in range(4):
                            kd = half * 4 + kk
                            mi = t.transpose(p[kk][0:128, 0:128],
                                             x_st[:, j * D + kd * 128: j * D + (kd + 1) * 128],
                                             eye_f[:])
                        cT.inc(mi)
                    B["tensor"](tX)
                    ct_g = cT.n

                    def vX(v, j=j, half=half, g=g, ct_g=ct_g):
                        v.wait_ge(cT.h, ct_g)
                        last = None
                        for kk in range(4):
                            kd = half * 4 + kk
                            last = v.tensor_copy(
                                xT[:, kd * ST + j * 128: kd * ST + (j + 1) * 128],
                                p[kk][0:128, 0:128])
                        n = cV.inc(last)
                        xg_v[g] = n
                    B["vector"](vX)
                x_rel_t = cT.n

                def tBc(t, s=s, last_xe=cV.n):
                    t.wait_ge(cV.h, last_xe)
                    for r_i, row in enumerate((r_nmean, r_rstd, r_rstdh)):
                        cT.inc(t.matmul(p[4 + r_i][:], ones_f[:],
                                        row[:, s * ST:(s + 1) * ST],
                                        start=True, stop=True))
                B["tensor"](tBc)
                ct_bc = cT.n

                def vN(v, ct_bc=ct_bc, ndq=norm_done_qs):
                    v.wait_ge(cT.h, ct_bc)
                    if ndq:
                        v.wait_ge(sXN.h, ndq)
                    for r_i in range(3):
                        v.tensor_copy(bA[r_i][:], p[4 + r_i][:])
                    last = None
                    for kd in range(NKD):
                        sl = slice(kd * ST, (kd + 1) * ST)
                        v.tensor_tensor(tb[:], xT[:, sl], bA[0][:], ALU.add)
                        v.tensor_tensor(xnb_st[:, sl], tb[:], bA[1][:], ALU.mult)
                        last = v.tensor_tensor(hb_st[:, sl], tb[:], bA[2][:], ALU.mult)
                    cV.inc(last)
                B["vector"](vN)
                cv_norm = cV.n

                def syN(sy, s=s, cv_norm=cv_norm):
                    sy.wait_ge(cV.h, cv_norm)
                    sXN.inc(sy.dma_start(out=xnb_s[s], in_=xnb_st[:]), 16)
                B["sync"](syN)
                norm_done_qs = sXN.n

                ev_s = [0] * NMU
                for mj in range(NMU):
                    def tU(t, mj=mj, cv_norm=cv_norm):
                        if mj == 0:
                            t.wait_ge(cV.h, cv_norm)
                        if mj >= 2:
                            t.wait_ge(cS.h, ev_s[mj - 2])
                        mi = None
                        for ki in range(NKD):
                            mi = t.matmul(p[mj % 2][:],
                                          winT[:, (ki * NMU + mj) * 128:(ki * NMU + mj + 1) * 128],
                                          hb_st[:, ki * ST:(ki + 1) * ST],
                                          start=(ki == 0), stop=(ki == NKD - 1))
                        cT.inc(mi)
                    B["tensor"](tU)
                    ct_u = cT.n

                    def sU(s_, mj=mj, ct_u=ct_u, sdq=stage_done_qs,
                           hdq=ufh_done_qs):
                        s_.wait_ge(cT.h, ct_u)
                        if mj == 0 and sdq:
                            s_.wait_ge(sSG.h, sdq)
                        if mj == 0 and hdq:
                            s_.wait_ge(sUF.h, hdq)
                        if mj == 8 and hdq:
                            s_.wait_ge(sUF.h, hdq)
                        bank = p[mj % 2]
                        if mj < NMJ:
                            sl = slice(mj * ST, (mj + 1) * ST)
                            hsl = slice((mj % 8) * ST, (mj % 8 + 1) * ST)
                            s_.activation(ubf_st[:, sl], bank[:], AF.Silu)
                            n = cS.inc(s_.activation(ufh_st[:, hsl], bank[:], AF.Silu))
                        else:
                            sl = slice((mj - NMJ) * ST, (mj - NMJ + 1) * ST)
                            n = cS.inc(s_.activation(szb_st[:, sl], bank[:], AF.Silu))
                        ev_s[mj] = n
                    B["scalar"](sU)

                    if mj == 7:
                        def syH1(sy, s=s, tgt=cS.n):
                            sy.wait_ge(cS.h, tgt)
                            sUF.inc(sy.dma_start(out=uf_s[s, :, 0:8 * ST],
                                                 in_=ufh_st[:]), 16)
                        B["sync"](syH1)
                        ufh_done_qs = sUF.n
                    elif mj == 15:
                        def syH2(sy, s=s, tgt=cS.n):
                            sy.wait_ge(cS.h, tgt)
                            sUF.inc(sy.dma_start(out=uf_s[s, :, 8 * ST:16 * ST],
                                                 in_=ufh_st[:]), 16)
                        B["sync"](syH2)
                        ufh_done_qs = sUF.n

                def syU(sy, s=s):
                    sy.wait_ge(cS.h, ev_s[NMU - 1])
                    sSG.inc(sy.dma_start(out=ubf_s[s], in_=ubf_st[:]), 16)
                    sSG.inc(sy.dma_start(out=szb_s[s], in_=szb_st[:]), 16)
                B["sync"](syU)
                stage_done_qs = sSG.n

        barrier()
        p12.__exit__(None, None, None)
        if upto <= 2:
            finish()
            return nc

        p34 = contextlib.ExitStack()
        p34.__enter__()
        wdtT = sb("wdtT", [128, NKI * NMJ * 128], BF16, p34)
        woutT = sb("woutT", [128, NKI * NMD * 128], BF16, p34)
        p3 = contextlib.ExitStack()
        with p3:
            NWB3 = 2
            wst3 = [sb(f"wst3_{i}", [128, INNER], F32, p3) for i in range(NWB3)]
            jobs = ([("dt", mj) for mj in range(NMJ)] +
                    [("out", mj) for mj in range(NMD)])
            strip_rel3 = [0] * len(jobs)
            grp_ev3 = [0] * (len(jobs) * 4)
            for jn, (kind, mj) in enumerate(jobs):
                def gP3(g, jn=jn, kind=kind, mj=mj):
                    if jn >= NWB3:
                        g.wait_ge(cV.h, strip_rel3[jn - NWB3])
                    src = (wdt_h[mj * 128:(mj + 1) * 128, :] if kind == "dt"
                           else wout_h[mj * 128:(mj + 1) * 128, :])
                    sW[jn % NWB3].inc(g.dma_start(out=wst3[jn % NWB3][:], in_=src), 16)
                B["gpsimd"](gP3)
                qg_j = sW[jn % NWB3].n

                for gi in range(4):
                    def tP3(t, jn=jn, gi=gi, qg_j=qg_j):
                        if gi == 0:
                            t.wait_ge(sW[jn % NWB3].h, qg_j)
                        gidx = jn * 4 + gi
                        if gidx >= 2:
                            t.wait_ge(cV.h, grp_ev3[gidx - 2])
                        mi = None
                        for kk in range(4):
                            ki = gi * 4 + kk
                            bank = p[(gidx % 2) * 4 + kk]
                            mi = t.transpose(bank[0:128, 0:128],
                                             wst3[jn % NWB3][:, ki * 128:(ki + 1) * 128],
                                             eye_f[:])
                        cT.inc(mi)
                    B["tensor"](tP3)
                    ct_g = cT.n

                    def vP3(v, jn=jn, gi=gi, kind=kind, mj=mj, ct_g=ct_g):
                        v.wait_ge(cT.h, ct_g)
                        gidx = jn * 4 + gi
                        last = None
                        for kk in range(4):
                            ki = gi * 4 + kk
                            bank = p[(gidx % 2) * 4 + kk]
                            if kind == "dt":
                                dst = (ki * NMJ + mj) * 128
                                last = v.tensor_copy(wdtT[:, dst:dst + 128],
                                                     bank[0:128, 0:128])
                            else:
                                dst = (ki * NMD + mj) * 128
                                last = v.tensor_copy(woutT[:, dst:dst + 128],
                                                     bank[0:128, 0:128])
                        n = cV.inc(last)
                        grp_ev3[gidx] = n
                    B["vector"](vP3)
                strip_rel3[jn] = cV.n

        barrier()
        if upto <= 3:
            finish()
            p34.__exit__(None, None, None)
            return nc

        p4 = contextlib.ExitStack()
        with p4:
            ubf4 = sb("ubf4", [128, NKI * ST], BF16, p4)
            uf4 = sb("uf4", [128, NKI * ST], F32, p4)
            szb4 = sb("szb4", [128, NKI * ST], BF16, p4)
            pbf4 = sb("pbf4", [128, NKI * ST], BF16, p4)
            lam4 = [sb(f"lam4_{i}", [128, ST], F32, p4) for i in range(2)]
            bt4 = sb("bt4", [128, ST], F32, p4)
            st4 = sb("st4", [128, ST], F32, p4)
            ya4 = sb("ya4", [128, NKD * ST], F32, p4)

            ya_done_qs = 0
            prod_v_prev = 0
            qs_sg_done = sSG.n
            qs_uf_done = sUF.n
            for s in range(NST):
                def sy4(sy, s=s, pvp=prod_v_prev):
                    if s == 0:
                        sy.wait_ge(sSG.h, qs_sg_done)
                        sy.wait_ge(sUF.h, qs_uf_done)
                    if pvp:
                        sy.wait_ge(cV.h, pvp)
                    sIN4.inc(sy.dma_start(out=ubf4[:], in_=ubf_s[s]), 16)
                    sIN4.inc(sy.dma_start(out=uf4[:], in_=uf_s[s]), 16)
                    sIN4.inc(sy.dma_start(out=szb4[:], in_=szb_s[s]), 16)
                B["sync"](sy4)
                in_done_qs = sIN4.n

                prod_v = [0] * NMJ
                lam_rel_v = [0] * NMJ
                sig_s = [0] * NMJ
                for mj in range(NMJ):
                    def t4(t, mj=mj, idq=in_done_qs):
                        if mj == 0:
                            t.wait_ge(sIN4.h, idq)
                        if mj >= 2:
                            t.wait_ge(cS.h, sig_s[mj - 2])
                        mi = None
                        for ki in range(NKI):
                            mi = t.matmul(p[mj % 2][:],
                                          wdtT[:, (ki * NMJ + mj) * 128:(ki * NMJ + mj + 1) * 128],
                                          ubf4[:, ki * ST:(ki + 1) * ST],
                                          start=(ki == 0), stop=(ki == NKI - 1))
                        cT.inc(mi)
                    B["tensor"](t4)
                    ct_l = cT.n

                    def s4(s_, mj=mj, ct_l=ct_l):
                        s_.wait_ge(cT.h, ct_l)
                        if mj >= 2:
                            s_.wait_ge(cV.h, lam_rel_v[mj - 2])
                        n = cS.inc(s_.activation(lam4[mj % 2][:], p[mj % 2][:],
                                                 AF.Sigmoid,
                                                 bias=bdt_sb[:, mj:mj + 1]))
                        sig_s[mj] = n
                    B["scalar"](s4)
                    cs_sig = cS.n

                    def v4(v, mj=mj, cs_sig=cs_sig):
                        v.wait_ge(cS.h, cs_sig)
                        lam = lam4[mj % 2]
                        sl = slice(mj * ST, (mj + 1) * ST)
                        v.tensor_single_scalar(bt4[:], lam[:], -1.0, ALU.mult)
                        v.drain()
                        v.tensor_single_scalar(bt4[:], bt4[:], 1.0, ALU.add)
                        v.drain()
                        v.tensor_tensor(bt4[:], bt4[:], uf4[:, sl], ALU.mult)
                        v.drain()
                        v.tensor_tensor_scan(st4[:], lam[:], bt4[:],
                                             state[:, mj:mj + 1], ALU.mult, ALU.add)
                        n = cV.inc(v.tensor_tensor(pbf4[:, sl], st4[:], szb4[:, sl],
                                                   ALU.mult))
                        prod_v[mj] = n
                        n = cV.inc(v.tensor_copy(state[:, mj:mj + 1],
                                                 st4[:, ST - 1:ST]))
                        lam_rel_v[mj] = n
                    B["vector"](v4)

                yev_v = [0] * NMD
                for mj in range(NMD):
                    def t4o(t, mj=mj):
                        if mj == 0:
                            t.wait_ge(cV.h, prod_v[NMJ - 1])
                        if mj >= 2:
                            t.wait_ge(cV.h, yev_v[mj - 2])
                        mi = None
                        for ki in range(NKI):
                            mi = t.matmul(p[2 + mj % 2][:],
                                          woutT[:, (ki * NMD + mj) * 128:(ki * NMD + mj + 1) * 128],
                                          pbf4[:, ki * ST:(ki + 1) * ST],
                                          start=(ki == 0), stop=(ki == NKI - 1))
                        cT.inc(mi)
                    B["tensor"](t4o)
                    ct_y = cT.n

                    def v4o(v, mj=mj, ct_y=ct_y, ydq=ya_done_qs):
                        v.wait_ge(cT.h, ct_y)
                        if mj == 0 and ydq:
                            v.wait_ge(sYA.h, ydq)
                        n = cV.inc(v.tensor_copy(ya4[:, mj * ST:(mj + 1) * ST],
                                                 p[2 + mj % 2][:]))
                        yev_v[mj] = n
                    B["vector"](v4o)

                def sy4o(sy, s=s):
                    sy.wait_ge(cV.h, yev_v[NMD - 1])
                    sYA.inc(sy.dma_start(out=ya_s[s], in_=ya4[:]), 16)
                B["sync"](sy4o)
                ya_done_qs = sYA.n
                prod_v_prev = prod_v[NMJ - 1]

        barrier()
        p34.__exit__(None, None, None)
        if upto <= 4:
            finish()
            return nc

        acc = sb("acc", [128, NST * NMD * ST], F32)
        qs_ya_done = sYA.n
        p5 = contextlib.ExitStack()
        with p5:
            x5 = sb("x5", [128, 4 * D], F32, p5)
            xT5 = sb("xT5", [128, NKD * ST], F32, p5)
            ya5 = sb("ya5", [128, NKD * ST], F32, p5)
            rab = sb("rab", [128, ST], F32, p5)
            rbb = sb("rbb", [128, ST], F32, p5)
            rowa5 = sb("rowa5", [1, TOK], F32, p5)
            rowb5 = sb("rowb5", [1, TOK], F32, p5)
            masks5 = sb("masks5", [NE, TOK], F32, p5)

            def sy5pre(sy):
                sy.wait_ge(sRT[0].h, qs_masks_rt[0])
                sy.wait_ge(sRT[1].h, qs_masks_rt[1])
                sPRE.inc(sy.dma_start(out=rowa5[:], in_=rows_s[0:1, :]), 16)
                sPRE.inc(sy.dma_start(out=rowb5[:], in_=rows_s[1:2, :]), 16)
                sPRE.inc(sy.dma_start(out=masks5[:], in_=mask_s), 16)
            B["sync"](sy5pre)
            qs5pre = sPRE.n

            acc_v = 0
            for c in range(NST):
                def sy5(sy, c=c, av=acc_v):
                    if c == 0:
                        sy.wait_ge(sYA.h, qs_ya_done)
                    if av:
                        sy.wait_ge(cV.h, av)
                    sP5.inc(sy.dma_start(out=x5[:].rearrange("q (j d) -> q j d", j=4),
                                 in_=x_h[c * ST:(c + 1) * ST, :].rearrange(
                                     "(j q) d -> q j d", q=128)), 16)
                    sP5.inc(sy.dma_start(out=ya5[:], in_=ya_s[c]), 16)
                B["sync"](sy5)
                qs5 = sP5.n

                xg5_v = [0] * 8
                for g in range(8):
                    j, half = g // 2, g % 2
                    def tX5(t, j=j, half=half, g=g, qs5=qs5):
                        if g == 0:
                            t.wait_ge(sP5.h, qs5)
                        if g >= 1:
                            t.wait_ge(cV.h, xg5_v[g - 1])
                        mi = None
                        for kk in range(4):
                            kd = half * 4 + kk
                            mi = t.transpose(p[kk][0:128, 0:128],
                                             x5[:, j * D + kd * 128: j * D + (kd + 1) * 128],
                                             eye_f[:])
                        cT.inc(mi)
                    B["tensor"](tX5)
                    ct_g5 = cT.n

                    def vX5(v, j=j, half=half, g=g, ct_g5=ct_g5):
                        v.wait_ge(cT.h, ct_g5)
                        last = None
                        for kk in range(4):
                            kd = half * 4 + kk
                            last = v.tensor_copy(
                                xT5[:, kd * ST + j * 128: kd * ST + (j + 1) * 128],
                                p[kk][0:128, 0:128])
                        n = cV.inc(last)
                        xg5_v[g] = n
                    B["vector"](vX5)

                def t5b(t, c=c, last_xe=cV.n, qs5pre=qs5pre):
                    t.wait_ge(cV.h, last_xe)
                    if c == 0:
                        t.wait_ge(sPRE.h, qs5pre)
                    cT.inc(t.matmul(p[4][:], ones_f[:],
                                    rowa5[:, c * ST:(c + 1) * ST],
                                    start=True, stop=True))
                    cT.inc(t.matmul(p[5][:], ones_f[:],
                                    rowb5[:, c * ST:(c + 1) * ST],
                                    start=True, stop=True))
                B["tensor"](t5b)
                ct5b = cT.n

                def v5b(v, ct5b=ct5b):
                    v.wait_ge(cT.h, ct5b)
                    v.tensor_copy(rab[:], p[4][:])
                    cV.inc(v.tensor_copy(rbb[:], p[5][:]))
                B["vector"](v5b)
                cv5b = cV.n

                def t5m(t, c=c, cv5b=cv5b):
                    t.wait_ge(cV.h, cv5b)
                    for m in range(NMD):
                        cT.inc(t.matmul(p[m][:], b2_sb[:, m * 128:(m + 1) * 128],
                                        masks5[:, c * ST:(c + 1) * ST],
                                        start=True, stop=True))
                B["tensor"](t5m)
                ct5m = cT.n

                def v5m(v, c=c, ct5m=ct5m):
                    v.wait_ge(cT.h, ct5m)
                    for m in range(NMD):
                        col = (c * NMD + m) * ST
                        a = acc[:, col:col + ST]
                        v.tensor_tensor(a, xT5[:, m * ST:(m + 1) * ST], rab[:],
                                        ALU.mult)
                        v.tensor_tensor(a, a, rbb[:], ALU.add)
                        v.tensor_tensor(a, a, p[m][:], ALU.add)
                        cV.inc(v.tensor_tensor(a, a, ya5[:, m * ST:(m + 1) * ST],
                                               ALU.add))
                B["vector"](v5m)
                acc_v = cV.n

        barrier()
        if upto <= 5:
            finish()
            return nc

        p6 = contextlib.ExitStack()
        with p6:
            w1sb = sb("w1sb", [128, NKD * DFF], BF16, p6)
            h1 = sb("h1", [128, NMF * ST], BF16, p6)
            w2sb = [sb(f"w2sb{i}", [128, 4 * D], BF16, p6) for i in range(2)]
            xnb6 = [sb(f"xnb6_{i}", [128, NKD * ST], BF16, p6) for i in range(2)]
            mkrow = sb("mkrow", [1, TOK], BF16, p6)
            mkb = [sb(f"mkb{i}", [128, ST], BF16, p6) for i in range(2)]
            tmp6 = sb("tmp6", [128, ST], F32, p6)

            w1_rel_t = 0
            xnb_rel_t = [0, 0]
            mkrow_rel_t = [0]
            w2_rel_t = {}
            evac_all_v = 0

            for e in range(NE):
                def g6w1(g, e=e, rel=w1_rel_t):
                    if rel:
                        g.wait_ge(cT.h, rel)
                    for ki in range(NKD):
                        sW1.inc(g.dma_start(out=w1sb[:, ki * DFF:(ki + 1) * DFF],
                                            in_=w1_h[e, ki * 128:(ki + 1) * 128, :]), 16)
                B["gpsimd"](g6w1)
                qg_w1 = sW1.n

                def sy6m(g, e=e, rel=mkrow_rel_t[0]):
                    if rel:
                        g.wait_ge(cT.h, rel)
                    if e == 0:
                        g.wait_ge(sRT[0].h, sRT[0].n)
                        g.wait_ge(sRT[1].h, sRT[1].n)
                    sMK.inc(g.dma_start(out=mkrow[:], in_=mask_s[e:e + 1, :]), 16)
                B["gpsimd"](sy6m)
                qs_mrow = sMK.n

                for c in range(NST):
                    u = e * NST + c

                    def sy6(sy, c=c, u=u, rel=xnb_rel_t[u % 2]):
                        if rel:
                            sy.wait_ge(cT.h, rel)
                        if u < 2:
                            sy.wait_ge(sXN.h, sXN.n)
                        sXB[u % 2].inc(sy.dma_start(out=xnb6[u % 2][:],
                                                    in_=xnb_s[c]), 16)
                    B["sync"](sy6)
                    qs_xnb = sXB[u % 2].n

                    def t6m(t, e=e, c=c, eav=evac_all_v, qs_mrow=qs_mrow):
                        if eav:
                            t.wait_ge(cV.h, eav)
                        if c == 0:
                            t.wait_ge(sMK.h, qs_mrow)
                        cT.inc(t.matmul(p[2][:], ones_b[:],
                                        mkrow[:, c * ST:(c + 1) * ST],
                                        start=True, stop=True))
                    B["tensor"](t6m)
                    ct6m = cT.n
                    if c == NST - 1:
                        mkrow_rel_t[0] = cT.n

                    def v6m(v, u=u, ct6m=ct6m):
                        v.wait_ge(cT.h, ct6m)
                        cV.inc(v.tensor_copy(mkb[u % 2][:], p[2][:]))
                    B["vector"](v6m)

                    h1_s = [0] * NMF
                    for mj in range(NMF):
                        def t6p1(t, u=u, mj=mj, qg_w1=qg_w1, qs_xnb=qs_xnb):
                            if mj == 0:
                                t.wait_ge(sW1.h, qg_w1)
                                t.wait_ge(sXB[u % 2].h, qs_xnb)
                            if mj >= 2:
                                t.wait_ge(cS.h, h1_s[mj - 2])
                            mi = None
                            for ki in range(NKD):
                                mi = t.matmul(p[mj % 2][:],
                                              w1sb[:, ki * DFF + mj * 128: ki * DFF + (mj + 1) * 128],
                                              xnb6[u % 2][:, ki * ST:(ki + 1) * ST],
                                              start=(ki == 0), stop=(ki == NKD - 1))
                            cT.inc(mi)
                        B["tensor"](t6p1)
                        ct_p1 = cT.n

                        def s6(s_, e=e, mj=mj, ct_p1=ct_p1):
                            s_.wait_ge(cT.h, ct_p1)
                            col = e * NMF + mj
                            n = cS.inc(s_.activation(h1[:, mj * ST:(mj + 1) * ST],
                                                     p[mj % 2][:], AF.Relu,
                                                     bias=b1_sb[:, col:col + 1]))
                            h1_s[mj] = n
                        B["scalar"](s6)
                    if c == NST - 1:
                        w1_rel_t = cT.n
                    xnb_rel_t[u % 2] = cT.n

                    for kg in range(8):
                        def g6w2(g, e=e, u=u, kg=kg):
                            key = (u - 1, 6 + kg) if kg < 2 else (u, kg - 2)
                            if key in w2_rel_t:
                                g.wait_ge(cT.h, w2_rel_t[key])
                            sW2[kg % 2].inc(g.dma_start(
                                out=w2sb[kg % 2][:].rearrange("q (kc d) -> q kc d", kc=4),
                                in_=w2_h[e, kg * 512:(kg + 1) * 512, :].rearrange(
                                    "(kc q) d -> q kc d", q=128)), 16)
                        B["gpsimd"](g6w2)
                        qg_w2 = sW2[kg % 2].n

                        def t6p2(t, kg=kg, qg_w2=qg_w2, cs_h1=cS.n, cv_now=cV.n):
                            t.wait_ge(sW2[kg % 2].h, qg_w2)
                            if kg == 0:
                                t.wait_ge(cS.h, cs_h1)
                                t.wait_ge(cV.h, cv_now)
                            mi = None
                            for kc in range(4):
                                kt = kg * 4 + kc
                                for m in range(NMD):
                                    mi = t.matmul(p[m][:],
                                                  w2sb[kg % 2][:, kc * D + m * 128: kc * D + (m + 1) * 128],
                                                  h1[:, kt * ST:(kt + 1) * ST],
                                                  start=(kt == 0), stop=(kt == 31))
                            cT.inc(mi)
                        B["tensor"](t6p2)
                        w2_rel_t[(u, kg)] = cT.n
                    ct_p2 = cT.n

                    def v6e(v, c=c, u=u, ct_p2=ct_p2):
                        v.wait_ge(cT.h, ct_p2)
                        last = None
                        for m in range(NMD):
                            col = (c * NMD + m) * ST
                            v.tensor_tensor(tmp6[:], p[m][:], mkb[u % 2][:], ALU.mult)
                            last = v.tensor_tensor(acc[:, col:col + ST],
                                                   acc[:, col:col + ST],
                                                   tmp6[:], ALU.add)
                        cV.inc(last)
                    B["vector"](v6e)
                    evac_all_v = cV.n

        barrier()
        if upto <= 6:
            finish()
            return nc

        out_dst = o7_s if p7_mode == "noout" else out_h
        p7 = contextlib.ExitStack()
        with p7:
            if p7_mode == "noxpose":
                def syNX(sy):
                    for ti in range(NT):
                        qO[ti % 2].inc(sy.dma_start(
                            out=out_dst[ti * 128:(ti + 1) * 128, :],
                            in_=acc[:, ti * 1024:(ti + 1) * 1024]), 16)
                    sy.wait_ge(qO[0].h, qO[0].n)
                    sy.wait_ge(qO[1].h, qO[1].n)
                B["sync"](syNX)
                finish()
                return nc
            ostg = sb("ostg", [128, NT * D], F32, p7)
            o_gv = [0] * (NT * 2)
            for ti in range(NT):
                c, j = ti // 4, ti % 4
                for g in range(2):
                    gidx = ti * 2 + g

                    def t7(t, ti=ti, c=c, j=j, g=g, gidx=gidx):
                        if gidx >= 2:
                            t.wait_ge(cS.h, o_gv[gidx - 2])
                        mi = None
                        for kk in range(4):
                            m = g * 4 + kk
                            col = (c * NMD + m) * ST + j * 128
                            mi = t.transpose(p[(gidx % 2) * 4 + kk][0:128, 0:128],
                                             acc[:, col:col + 128], eye_f[:])
                        cT.inc(mi)
                    B["tensor"](t7)
                    ct7g = cT.n

                    def s7(s_, ti=ti, g=g, gidx=gidx, ct7g=ct7g):
                        s_.wait_ge(cT.h, ct7g)
                        last = None
                        for kk in range(4):
                            m = g * 4 + kk
                            last = s_.activation(
                                ostg[:, ti * D + m * 128: ti * D + (m + 1) * 128],
                                p[(gidx % 2) * 4 + kk][0:128, 0:128], AF.Copy)
                        n = cS.inc(last)
                        o_gv[gidx] = n
                    B["scalar"](s7)

            def syOut(sy):
                sy.wait_ge(cS.h, o_gv[NT * 2 - 1])
                for ti in range(NT):
                    qO[ti % 2].inc(sy.dma_start(
                        out=out_dst[ti * 128:(ti + 1) * 128, :],
                        in_=ostg[:, ti * D:(ti + 1) * D]), 16)
                sy.wait_ge(qO[0].h, qO[0].n)
                sy.wait_ge(qO[1].h, qO[1].n)
            B["sync"](syOut)

            def syF(sy):
                for sm in (qC, qE, sXS, sXN, sUF, sSG, sRC, sIN4, sYA, sP5,
                           sPRE, sW1, sMK):
                    sy.wait_ge(sm.h, sm.n)
                for sm in sX + sW + sRT + sXB + sW2:
                    sy.wait_ge(sm.h, sm.n)
            B["sync"](syF)
            barrier()

    return nc


def kernel(**inputs):
    nc = build(debug=False)
    x = np.ascontiguousarray(inputs["L0_attn_x"], dtype=np.float32)
    base = {
        "w_norm": np.ascontiguousarray(inputs["L0_attn_w_norm"], dtype=np.float32),
        "W_in": np.ascontiguousarray(inputs["L0_attn_W_in"], dtype=np.float32),
        "W_dt": np.ascontiguousarray(inputs["L0_attn_W_dt"], dtype=np.float32),
        "b_dt": np.ascontiguousarray(inputs["L0_attn_b_dt"], dtype=np.float32),
        "W_out": np.ascontiguousarray(inputs["L0_attn_W_out"], dtype=np.float32),
        "W1": np.ascontiguousarray(inputs["L0_ffn_W1"], dtype=np.float32),
        "b1": np.ascontiguousarray(inputs["L0_ffn_b1"], dtype=np.float32),
        "W2": np.ascontiguousarray(inputs["L0_ffn_W2"], dtype=np.float32),
        "b2": np.ascontiguousarray(inputs["L0_ffn_b2"], dtype=np.float32),
        "eye": np.eye(128, dtype=np.float32),
    }
    in_maps = [dict(base, x=np.ascontiguousarray(x[b])) for b in range(8)]
    res = run_bass_kernel_spmd(nc, in_maps, core_ids=list(range(8)))
    outs = []
    for r in res.results:
        a = np.asarray(r["out"])
        a = a.reshape(128, NST, NMD, ST)
        outs.append(np.ascontiguousarray(
            a.transpose(1, 3, 2, 0).reshape(TOK, D)))
    return np.stack(outs, axis=0)
